# version 02
# baseline (speedup 1.0000x reference)
import numpy as np
import ml_dtypes
from contextlib import ExitStack

import concourse.bass as bass
import concourse.mybir as mybir
from concourse.bass_utils import run_bass_kernel_spmd

F32 = mybir.dt.float32
BF16 = mybir.dt.bfloat16
I32 = mybir.dt.int32
U32 = mybir.dt.uint32
AF = mybir.ActivationFunctionType
ALU = mybir.AluOpType
AX = mybir.AxisListType

P = 128
D = 1024
T = 2048
NTC = 16
NDC = 8
NE = 8
FH = 4096
NRING = 4
STAB_C = 48.0

FULL_PLAN = ["ln", "moe0", "ln", "attn", "ln", "moe1",
             "l1prep", "ln", "l1ffn", "lng", "l1attn", "ln", "l1ffn"]


class Sem:
    def __init__(self, stack, nc, name):
        self.h = stack.enter_context(nc.semaphore(name))
        self.v = 0

    def inc(self, inst, n=1):
        inst.then_inc(self.h, n)
        self.v += n
        return self.v


class Ctx:
    pass


def _sb(ctx, name, shape, dt=F32, stack=None):
    return (stack or ctx.stack).enter_context(ctx.nc.sbuf_tensor(name, shape, dt))


def _ps(ctx, name, shape, dt=F32, stack=None):
    return (stack or ctx.stack).enter_context(ctx.nc.psum_tensor(name, shape, dt))


def unit_schedule(plan):
    units = []

    def moe_units(call):
        nonlocal_units = units
        for e in range(NE):
            for g in range(4):
                nonlocal_units += [("w1", call, e, g, "h1a"), ("w1", call, e, g, "h2a"),
                          ("w1", call, e, g, "h1b"), ("w1", call, e, g, "h2b"),
                          ("w2", call, e, g, "a"), ("w2", call, e, g, "b")]

    for st in plan:
        if st == "moe0":
            moe_units(0)
        elif st == "moe1":
            moe_units(1)
        elif st == "attn":
            units += [("wv", "a"), ("wv", "b"),
                      ("wod", 0, "a"), ("wod", 0, "b"),
                      ("wod", 1, "a"), ("wod", 1, "b")]
        elif st == "l1attn":
            units += [("qt", "a"), ("qt", "b")]
    return units


def unit_src_ap(ctx, u):
    if u[0] == "w1":
        _, call, e, g, part = u
        r = ctx.d_w1T.ap()[e].rearrange("(c p) f -> p c f", p=P)
        f0 = {"h1a": g * 1024, "h2a": FH + g * 1024,
              "h1b": g * 1024 + 512, "h2b": FH + g * 1024 + 512}[part]
        return r[:, :, f0:f0 + 512]
    if u[0] == "w2":
        _, call, e, g, half = u
        r = ctx.d_w2T.ap()[e].rearrange("(c p) o -> p c o", p=P)
        o0 = 0 if half == "a" else 512
        return r[:, g * 8:(g + 1) * 8, o0:o0 + 512]
    if u[0] == "wod":
        _, jhalf, half = u
        o0 = 0 if half == "a" else 512
        return ctx.d_WoTd.ap()[:, jhalf * 8:(jhalf + 1) * 8, o0:o0 + 512]
    name, half = u
    dram = {"wv": ctx.d_WvT, "qt": ctx.d_QT}[name]
    r = dram.ap().rearrange("(c p) o -> p c o", p=P)
    o0 = 0 if half == "a" else 512
    return r[:, :, o0:o0 + 512]


def emit_unit_dmas(ctx, g, lo, hi):
    for k in range(lo, hi):
        if k >= NRING:
            g.wait_ge(ctx.s_uf.h, k - NRING + 1)
        ctx.s_un.inc(g.dma_start(ctx.ring[k % NRING][:, :, :],
                                 unit_src_ap(ctx, ctx.units[k])), 16)
    ctx.units_issued = hi


def uwait(eng, ctx, k):
    eng.wait_ge(ctx.s_un.h, 16 * (k + 1))


def uf_free(ctx, n=1):
    ctx.pending_uf += n


def attach_uf(ctx, ins):
    if ctx.pending_uf:
        ins.then_inc(ctx.s_uf.h, ctx.pending_uf)
        ctx.s_uf.v += ctx.pending_uf
        ctx.pending_uf = 0


def emit_ln(ctx, xlntok=None):
    nc = ctx.nc
    with ExitStack() as ph:
        if xlntok is None:
            xlntok = _sb(ctx, f"xlntok{ctx.uid()}", [P, NTC, D], BF16, stack=ph)
        sqscr = _sb(ctx, f"sqscr{ctx.uid()}", [P, D], F32, stack=ph)
        pt_ps = _ps(ctx, f"ptps{ctx.uid()}", [P, 2, 8, 128], BF16, stack=ph)
        with nc.Block() as block:
            dv0 = ctx.s_dv.v
            ac0 = ctx.s_ac.v
            pe0 = ctx.s_pe.v

            @block.vector
            def _(v):
                v.wait_ge(ctx.s_ld.h, ctx.ld_setup)
                for tc in range(NTC):
                    ins = v.reduce_sum(ctx.s1buf[:, tc:tc + 1], ctx.X[:, tc, :], AX.X)
                ctx.s_dv.inc(ins)

            @block.scalar
            def _(sc):
                sc.wait_ge(ctx.s_ld.h, ctx.ld_setup)
                for tc in range(NTC):
                    sc.activation(sqscr[:, :], ctx.X[:, tc, :], AF.Square,
                                  accum_out=ctx.s2buf[:, tc:tc + 1])
                sc.wait_ge(ctx.s_dv.h, dv0 + 1)
                for tc in range(NTC):
                    sc.mul(ctx.mstr[:, tc * 8:tc * 8 + 1], ctx.s1buf[:, tc:tc + 1], 1.0 / D)
                for tc in range(NTC):
                    ins = sc.activation(ctx.msqbuf[:, tc:tc + 1],
                                        ctx.mstr[:, tc * 8:tc * 8 + 1], AF.Square)
                ctx.s_ac.inc(ins)

            @block.vector
            def _(v):
                v.wait_ge(ctx.s_ac.h, ac0 + 1)
                for tc in range(NTC):
                    ins = v.scalar_tensor_tensor(ctx.varbuf[:, tc:tc + 1],
                                                 ctx.s2buf[:, tc:tc + 1], 1.0 / D,
                                                 ctx.msqbuf[:, tc:tc + 1],
                                                 ALU.mult, ALU.subtract)
                ctx.s_dv.inc(ins)

            @block.scalar
            def _(sc):
                sc.wait_ge(ctx.s_dv.h, dv0 + 2)
                for tc in range(NTC):
                    ins = sc.activation(ctx.stdbuf[:, tc:tc + 1], ctx.varbuf[:, tc:tc + 1],
                                        AF.Sqrt, bias=ctx.epsc[:, :], scale=1.0)
                ctx.s_ac.inc(ins)

            @block.vector
            def _(v):
                v.wait_ge(ctx.s_ac.h, ac0 + 2)
                for tc in range(NTC):
                    v.reciprocal(ctx.rstdstr[:, tc * 8:tc * 8 + 1], ctx.stdbuf[:, tc:tc + 1])
                for tc in range(NTC):
                    ctx.s_dv.inc(v.tensor_scalar(xlntok[:, tc, :], ctx.X[:, tc, :],
                                                 ctx.mstr[:, tc * 8:tc * 8 + 1],
                                                 ctx.rstdstr[:, tc * 8:tc * 8 + 1],
                                                 ALU.subtract, ALU.mult))

            @block.tensor
            def _(t):
                t.wait_ge(ctx.s_cn.h, 32)
                for tc in range(NTC):
                    t.wait_ge(ctx.s_dv.h, dv0 + 3 + tc)
                    if tc >= 2:
                        t.wait_ge(ctx.s_ac.h, ac0 + 2 + (tc - 1))
                    for dc in range(NDC):
                        ins = t.transpose(pt_ps[:, tc % 2, dc, :],
                                          xlntok[:, tc, dc * 128:(dc + 1) * 128],
                                          ctx.ident_bf[:, :])
                        if tc == 0 and dc == 0:
                            attach_uf(ctx, ins)
                    ctx.s_pe.inc(ins)

            @block.scalar
            def _(sc):
                for tc in range(NTC):
                    sc.wait_ge(ctx.s_pe.h, pe0 + tc + 1)
                    ctx.s_ac.inc(sc.copy(ctx.xlnT[:, :, tc * 128:(tc + 1) * 128],
                                         pt_ps[:, tc % 2, :, :]))


def emit_moe(ctx, call):
    nc = ctx.nc
    u0 = ctx.unit_base[("moe", call)]
    ctx.m1_i = 0
    ctx.m1_i_act = 0
    ctx.m2_i = 0
    ctx.m2_i_dve = 0
    ctx.m1_base = ctx.s_m1.v
    ctx.e1_base = ctx.s_e1.v
    ctx.hr_base = ctx.s_hr.v
    ctx.m2_base = ctx.s_m2.v
    ctx.e2_base = ctx.s_e2.v
    with ExitStack() as ph:
        uid = ctx.uid()
        rb_bc = _sb(ctx, f"rbbc{uid}", [P, 8], F32, stack=ph)
        b1cols = _sb(ctx, f"b1c{uid}", [P, NE, 64], F32, stack=ph)
        b2sb = _sb(ctx, f"b2sb{uid}", [8, D], F32, stack=ph)
        wstr = _sb(ctx, f"wstr{uid}", [P, NTC, 8, 8], F32, stack=ph)
        ctx.wstr = wstr
        sub = ExitStack()
        R = Ctx()
        R.lsb = _sb(ctx, f"lsb{uid}", [P, NTC, 8], F32, stack=sub)
        R.mx8 = _sb(ctx, f"mx8{uid}", [P, NTC, 8], F32, stack=sub)
        R.idxu = _sb(ctx, f"idxu{uid}", [P, NTC, 8], U32, stack=sub)
        R.idxf = _sb(ctx, f"idxf{uid}", [P, NTC, 8], F32, stack=sub)
        R.negv1 = _sb(ctx, f"negv1{uid}", [P, NTC], F32, stack=sub)
        R.e2 = _sb(ctx, f"e2{uid}", [P, NTC], F32, stack=sub)
        R.den = _sb(ctx, f"den{uid}", [P, NTC], F32, stack=sub)
        R.idx1s = _sb(ctx, f"idx1s{uid}", [P, NTC * 8], F32, stack=sub)
        R.recs = _sb(ctx, f"recs{uid}", [P, NTC * 8], F32, stack=sub)
        R.sw0s = _sb(ctx, f"sw0s{uid}", [P, NTC * 8], F32, stack=sub)
        R.sw1s = _sb(ctx, f"sw1s{uid}", [P, NTC * 8], F32, stack=sub)
        R.w8a = _sb(ctx, f"w8a{uid}", [P, NTC, 8], F32, stack=sub)
        R.w8b = _sb(ctx, f"w8b{uid}", [P, NTC, 8], F32, stack=sub)
        R.w8c = _sb(ctx, f"w8c{uid}", [P, NTC, 8], F32, stack=sub)
        R.wT8 = _sb(ctx, f"wT8{uid}", [8, T], F32, stack=sub)
        sub.close()
        ctx.hbuf = _sb(ctx, f"hbuf{uid}", [P, 8, T], BF16, stack=ph)
        ctx.s1sb = _sb(ctx, f"s1sb{uid}", [P, 2, T], BF16, stack=ph)
        ctx.h2sb = _sb(ctx, f"h2sb{uid}", [P, 2, T], BF16, stack=ph)
        ctx.b1cols = b1cols
        Q = [_ps(ctx, f"q{i}_{uid}", [P, 1024], F32, stack=ph) for i in range(4)]
        ctx.curQ = Q

        with nc.Block() as block:
            pe0 = ctx.s_pe.v
            dv0 = ctx.s_dv.v
            ac0 = ctx.s_ac.v
            dvb0 = ctx.s_dvb.v
            ld0 = ctx.s_ld.v

            @block.sync
            def _(s):
                ctx.s_ld.inc(s.dma_start(rb_bc[:, :], ctx.d_rb_bc.ap()), 16)
                ctx.s_ld.inc(s.dma_start(b1cols[:, :, :], ctx.d_b1cols.ap()), 16)
                ctx.s_ld.inc(s.dma_start(b2sb[:, :], ctx.d_b2.ap()), 16)
                if DEBUG_MOE and call == 0:
                    s.wait_ge(ctx.s_dv.h, dv0 + 17)
                    ctx.s_ld.inc(s.dma_start(ctx.d_dbg_lsb.ap(), R.lsb[:, :, :]), 16)
                    ctx.s_ld.inc(s.dma_start(ctx.d_dbg_mx8.ap(), R.mx8[:, :, :]), 16)
                    ctx.s_ld.inc(s.dma_start(ctx.d_dbg_idx.ap(), R.idxf[:, :, :]), 16)
                    ctx.s_ld.inc(s.dma_start(ctx.d_dbg_w8c.ap(), R.w8c[:, :, :]), 16)
                    ctx.s_ld.inc(s.dma_start(ctx.d_dbg_wstr.ap(), wstr[:, :, :, :]), 16)

            @block.gpsimd
            def _(g):
                emit_unit_dmas(ctx, g, ctx.units_issued, u0 + 24 * NE)

            @block.tensor
            def _(t):
                t.wait_ge(ctx.s_cn.h, 32)
                t.wait_ge(ctx.s_ld.h, ctx.ld_setup)
                for tc in range(NTC):
                    reg = Q[0][:, (tc % 2) * 512:(tc % 2) * 512 + 8]
                    if tc >= 2:
                        t.wait_ge(ctx.s_dv.h, dv0 + (tc - 1))
                    for dc in range(NDC):
                        ins = t.matmul(reg, ctx.xlnT[:, dc, tc * 128:(tc + 1) * 128],
                                       ctx.rwT_bf[:, dc, :], start=(dc == 0),
                                       stop=(dc == NDC - 1))
                        if dc == 0:
                            attach_uf(ctx, ins)
                    ctx.s_pe.inc(ins)
                t.wait_ge(ctx.s_dv.h, dv0 + 17)
                for tc in range(NTC):
                    dstq = Q[2] if tc < 8 else Q[3]
                    ins = t.transpose(dstq[0:8, (tc % 8) * 128:(tc % 8 + 1) * 128],
                                      R.w8c[:, tc, :], ctx.ident_f32[:, :])
                    if tc in (7, 15):
                        ctx.s_pe.inc(ins)
                t.wait_ge(ctx.s_ac.h, ac0 + 2)
                for tc in range(NTC):
                    reg = Q[tc % 2]
                    if tc >= 2:
                        t.wait_ge(ctx.s_dv.h, dv0 + 17 + (tc - 1))
                    for ns in range(2):
                        ins = t.matmul(reg[:, ns * 512:(ns + 1) * 512],
                                       R.wT8[0:8, tc * 128:(tc + 1) * 128],
                                       b2sb[0:8, ns * 512:(ns + 1) * 512],
                                       start=True, stop=True)
                    ctx.s_pe.inc(ins)
                ctx.moe_guard = (ctx.s_dv.h, dv0 + 33, ctx.s_ac.h, ac0 + 2)
                for e in range(NE):
                    emit_expert_pe(ctx, t, e, u0 + 24 * e)

            @block.vector
            def _(v):
                v.wait_ge(ctx.s_ld.h, ld0 + 48)
                for tc in range(NTC):
                    v.wait_ge(ctx.s_pe.h, pe0 + tc + 1)
                    reg = Q[0][:, (tc % 2) * 512:(tc % 2) * 512 + 8]
                    ctx.s_dv.inc(v.tensor_tensor(R.lsb[:, tc, :], reg, rb_bc[:, :],
                                                 ALU.add))
                for tc in range(NTC):
                    v.max(R.mx8[:, tc, :], R.lsb[:, tc, :])
                for tc in range(NTC):
                    ins = v.max_index(R.idxu[:, tc, :], R.mx8[:, tc, :], R.lsb[:, tc, :])
                ctx.s_dvb.inc(ins)
                v.wait_ge(ctx.s_ac.h, ac0 + 1)
                for tc in range(NTC):
                    v.tensor_scalar_add(R.den[:, tc:tc + 1], R.e2[:, tc:tc + 1], 1.0)
                for tc in range(NTC):
                    v.reciprocal(R.recs[:, tc * 8:tc * 8 + 1], R.den[:, tc:tc + 1])
                for tc in range(NTC):
                    v.tensor_scalar(R.sw0s[:, tc * 8:tc * 8 + 1],
                                    R.recs[:, tc * 8:tc * 8 + 1], 0.5, None, ALU.mult)
                for tc in range(NTC):
                    v.tensor_scalar(R.sw1s[:, tc * 8:tc * 8 + 1], R.e2[:, tc:tc + 1],
                                    R.recs[:, tc * 8:tc * 8 + 1], 0.5, ALU.mult, ALU.mult)
                for tc in range(NTC):
                    v.tensor_scalar(R.w8a[:, tc, :], ctx.iota8_bc[:, :],
                                    R.idxf[:, tc, 0:1],
                                    R.sw0s[:, tc * 8:tc * 8 + 1], ALU.is_equal, ALU.mult)
                for tc in range(NTC):
                    v.tensor_scalar(R.w8b[:, tc, :], ctx.iota8_bc[:, :],
                                    R.idx1s[:, tc * 8:tc * 8 + 1],
                                    R.sw1s[:, tc * 8:tc * 8 + 1], ALU.is_equal, ALU.mult)
                for tc in range(NTC):
                    v.tensor_tensor(wstr[:, tc, :, 0], R.w8a[:, tc, :],
                                    R.w8b[:, tc, :], ALU.add)
                for tc in range(NTC):
                    ins = v.tensor_tensor(R.w8c[:, tc, :], R.w8a[:, tc, :],
                                          R.w8b[:, tc, :], ALU.add)
                ctx.s_dv.inc(ins)
                for tc in range(NTC):
                    v.wait_ge(ctx.s_pe.h, pe0 + 19 + tc)
                    ctx.s_dv.inc(v.tensor_tensor(ctx.X[:, tc, :], ctx.X[:, tc, :],
                                                 Q[tc % 2][:, :], ALU.add))
                for e in range(NE):
                    emit_expert_dve(ctx, v, e)

            @block.scalar
            def _(sc):
                sc.wait_ge(ctx.s_ld.h, ld0 + 48)
                sc.wait_ge(ctx.s_dvb.h, dvb0 + 1)
                for tc in range(NTC):
                    sc.mul(R.negv1[:, tc:tc + 1], R.mx8[:, tc, 0:1], -1.0)
                for tc in range(NTC):
                    sc.copy(R.idxf[:, tc, :], R.idxu[:, tc, :])
                for tc in range(NTC):
                    sc.activation(R.e2[:, tc:tc + 1], R.mx8[:, tc, 1:2], AF.Exp,
                                  bias=R.negv1[:, tc:tc + 1])
                for tc in range(NTC):
                    ins = sc.copy(R.idx1s[:, tc * 8:tc * 8 + 1], R.idxf[:, tc, 1:2])
                ctx.s_ac.inc(ins)
                sc.wait_ge(ctx.s_pe.h, pe0 + 18)
                sc.copy(R.wT8[0:8, 0:1024], Q[2][0:8, :])
                ctx.s_ac.inc(sc.copy(R.wT8[0:8, 1024:2048], Q[3][0:8, :]))
                for e in range(NE):
                    emit_expert_act(ctx, sc, e)


def emit_expert_pe(ctx, t, e, ubase):
    Q = ctx.curQ
    for g in range(4):
        ub = ubase + 6 * g
        for c in range(8):
            for part in range(2):
                i = ctx.m1_i
                uk = ub + (0 if c < 4 else 2) + part
                uwait(t, ctx, uk)
                p0 = Q[(i % 2) * 2]
                p1 = Q[(i % 2) * 2 + 1]
                if i == 0:
                    t.wait_ge(ctx.moe_guard[0], ctx.moe_guard[1])
                elif i == 1:
                    t.wait_ge(ctx.moe_guard[2], ctx.moe_guard[3])
                else:
                    t.wait_ge(ctx.s_e1.h, ctx.e1_base + 2 * (i - 1))
                u = ctx.ring[uk % NRING]
                fcol = (c % 4) * 128
                for dc in range(NDC):
                    last = dc == NDC - 1
                    ins0 = t.matmul(p0[:, 0:512], u[:, dc, fcol:fcol + 128],
                                    ctx.xlnT[:, dc, 0:512], start=(dc == 0), stop=last)
                    if dc == 0:
                        attach_uf(ctx, ins0)
                    t.matmul(p0[:, 512:1024], u[:, dc, fcol:fcol + 128],
                             ctx.xlnT[:, dc, 512:1024], start=(dc == 0), stop=last)
                    t.matmul(p1[:, 0:512], u[:, dc, fcol:fcol + 128],
                             ctx.xlnT[:, dc, 1024:1536], start=(dc == 0), stop=last)
                    ins = t.matmul(p1[:, 512:1024], u[:, dc, fcol:fcol + 128],
                                   ctx.xlnT[:, dc, 1536:2048], start=(dc == 0), stop=last)
                ctx.s_m1.inc(ins)
                if (c, part) in ((3, 0), (3, 1), (7, 0), (7, 1)):
                    uf_free(ctx)
                ctx.m1_i += 1
        uwait(t, ctx, ub + 4)
        uwait(t, ctx, ub + 5)
        t.wait_ge(ctx.s_hr.h, ctx.hr_base + 8 * (4 * e + g + 1))
        ua = ctx.ring[(ub + 4) % NRING]
        ubf = ctx.ring[(ub + 5) % NRING]
        for tc in range(NTC):
            j = ctx.m2_i
            if j >= 2:
                t.wait_ge(ctx.s_e2.h, ctx.e2_base + (j - 1))
            for c in range(8):
                last = c == 7
                ins0 = t.matmul(Q[j % 2][:, 0:512], ctx.hbuf[:, c, tc * 128:(tc + 1) * 128],
                                ua[:, c, :], start=(c == 0), stop=last)
                if c == 0:
                    attach_uf(ctx, ins0)
                ins = t.matmul(Q[j % 2][:, 512:1024],
                               ctx.hbuf[:, c, tc * 128:(tc + 1) * 128],
                               ubf[:, c, :], start=(c == 0), stop=last)
            ctx.s_m2.inc(ins)
            if tc == NTC - 1:
                uf_free(ctx, 2)
            ctx.m2_i += 1
        t.wait_ge(ctx.s_e2.h, ctx.e2_base + ctx.m2_i)


def emit_expert_act(ctx, sc, e):
    Q = ctx.curQ
    for g in range(4):
        for c in range(8):
            for part in range(2):
                i = ctx.m1_i_act
                sc.wait_ge(ctx.s_m1.h, ctx.m1_base + i + 1)
                m = 8 * (4 * e + g) + c
                if m >= 2 and part == 0:
                    sc.wait_ge(ctx.s_hr.h, ctx.hr_base + m - 1)
                p0 = Q[(i % 2) * 2]
                p1 = Q[(i % 2) * 2 + 1]
                fcglob = g * 8 + c + (32 if part == 1 else 0)
                bias = ctx.b1cols[:, e, fcglob:fcglob + 1]
                dst = ctx.s1sb if part == 0 else ctx.h2sb
                fn = AF.Silu if part == 0 else AF.Identity
                i1 = sc.activation(dst[:, c % 2, 0:1024], p0[:, :], fn, bias=bias)
                i2 = sc.activation(dst[:, c % 2, 1024:2048], p1[:, :], fn, bias=bias)
                ctx.s_e1.inc(i1)
                ctx.s_e1.inc(i2)
                ctx.m1_i_act += 1


def emit_expert_dve(ctx, v, e):
    Q = ctx.curQ
    for g in range(4):
        for c in range(8):
            v.wait_ge(ctx.s_e1.h, ctx.e1_base + 32 * (4 * e + g) + 4 * c + 4)
            ctx.s_hr.inc(v.tensor_tensor(ctx.hbuf[:, c, :], ctx.s1sb[:, c % 2, :],
                                         ctx.h2sb[:, c % 2, :], ALU.mult))
        for tc in range(NTC):
            j = ctx.m2_i_dve
            v.wait_ge(ctx.s_m2.h, ctx.m2_base + j + 1)
            ctx.s_e2.inc(v.scalar_tensor_tensor(
                ctx.X[:, tc, :], Q[j % 2][:, :], ctx.wstr[:, tc, e, 0:1],
                ctx.X[:, tc, :], ALU.mult, ALU.add))
            ctx.m2_i_dve += 1


def emit_attn(ctx):
    nc = ctx.nc
    ub = ctx.unit_base[("attn",)]
    ctx.attn_ub = ub
    uwv_a, uwv_b = ctx.ring[ub % NRING], ctx.ring[(ub + 1) % NRING]
    uwo = [[ctx.ring[(ub + 2 + 2 * jh + oh) % NRING] for oh in range(2)] for jh in range(2)]
    with ExitStack() as ph:
        uid = ctx.uid()
        A = Ctx()
        A.pT = _sb(ctx, f"pT{uid}", [P, NDC, T], BF16, stack=ph)
        A.vhead = _sb(ctx, f"vhead{uid}", [P, NTC, 72], BF16, stack=ph)
        A.ebuf = _sb(ctx, f"ebuf{uid}", [P, 2, T], BF16, stack=ph)
        A.d2col = _sb(ctx, f"d2col{uid}", [P, NTC], F32, stack=ph)
        A.scolh = _sb(ctx, f"scolh{uid}", [P, NTC * 8], F32, stack=ph)
        A.wneg = _sb(ctx, f"wneg{uid}", [P, NTC], F32, stack=ph)
        A.wnegb = _sb(ctx, f"wnegb{uid}", [P, NTC], BF16, stack=ph)
        A.wrow_sb = _sb(ctx, f"wrow{uid}", [16, P], BF16, stack=ph)
        A.extL = ctx.xlnT[0:66, 0, :]
        A.extR = ctx.xlnT[0:66, 1, :]
        A.av_sb = ctx.xlnT[:, 2:4, :].rearrange("p a b -> p (a b)").bitcast(F32)
        A.zp0 = ctx.xlnT[0:1, 4:6, :].rearrange("p a b -> p (a b)").bitcast(F32)
        A.rz1 = ctx.xlnT[0:1, 6:8, :].rearrange("p a b -> p (a b)").bitcast(F32)
        bv_cols = _sb(ctx, f"bvc{uid}", [P, 8], F32, stack=ph)
        bo_bc = _sb(ctx, f"bobc{uid}", [P, D], F32, stack=ph)
        A.psA = _ps(ctx, f"psA{uid}", [P, T], F32, stack=ph)
        A.psB = _ps(ctx, f"psB{uid}", [P, T], F32, stack=ph)
        psA, psB = A.psA, A.psB

        with nc.Block() as block:
            pe0 = ctx.s_pe.v
            ac0 = ctx.s_ac.v
            ld0 = ctx.s_ld.v

            @block.sync
            def _(s):
                ctx.s_ld.inc(s.dma_start(bv_cols[:, :], ctx.d_bv_cols.ap()), 16)
                ctx.s_ld.inc(s.dma_start(bo_bc[:, :], ctx.d_bo_bc.ap()), 16)
                ctx.attn_ld = ctx.s_ld.v

            @block.gpsimd
            def _(g):
                emit_unit_dmas(ctx, g, ctx.units_issued, ub + 4)

            @block.tensor
            def _(t):
                uwait(t, ctx, ub + 0)
                uwait(t, ctx, ub + 1)
                for ec in range(NDC):
                    reg = psA if ec % 2 == 0 else psB
                    if ec >= 2:
                        t.wait_ge(ctx.s_ac.h, ac0 + (ec - 1))
                    usrc = uwv_a if ec < 4 else uwv_b
                    for dc in range(NDC):
                        last = dc == NDC - 1
                        for ns in range(4):
                            ins = t.matmul(reg[:, ns * 512:(ns + 1) * 512],
                                           usrc[:, dc, (ec % 4) * 128:(ec % 4 + 1) * 128],
                                           ctx.xlnT[:, dc, ns * 512:(ns + 1) * 512],
                                           start=(dc == 0), stop=last)
                            if dc == 0 and ns == 0:
                                attach_uf(ctx, ins)
                    ctx.s_pe.inc(ins)
                    if ec == NDC - 1:
                        uf_free(ctx, 2)

            @block.scalar
            def _(sc):
                sc.wait_ge(ctx.s_ld.h, ld0 + 16)
                for ec in range(NDC):
                    sc.wait_ge(ctx.s_pe.h, pe0 + ec + 1)
                    reg = psA if ec % 2 == 0 else psB
                    ctx.s_ac.inc(sc.activation(A.pT[:, ec, :], reg[:, :], AF.Identity,
                                               bias=bv_cols[:, ec:ec + 1]))

            if DEBUG_ATTN:
                @block.gpsimd
                def _(g):
                    g.wait_ge(ctx.s_ac.h, ac0 + 8)
                    ctx.s_ld.inc(g.dma_start(ctx.d_dbg_pT.ap(), A.pT[:, :, :]), 16)
                    ctx.attn_ld = ctx.s_ld.v

        for h in range(16):
            emit_attn_head(ctx, A, h)
            if DEBUG_ATTN and h == DEBUG_HEAD:
                with nc.Block() as blk2:
                    @blk2.sync
                    def _(s):
                        ctx.s_ld.inc(s.dma_start(ctx.d_dbg_d2.ap(), A.d2col[:, :]), 16)
                        ctx.s_ld.inc(s.dma_start(ctx.d_dbg_av.ap(), A.av_sb[0:65, :]), 16)
                    @blk2.gpsimd
                    def _(g):
                        ctx.s_ld.inc(g.dma_start(ctx.d_dbg_eb.ap(), A.ebuf[:, :, :]), 16)
                        ctx.s_ld.inc(g.dma_start(ctx.d_dbg_vh.ap(), A.vhead[:, :, :]), 16)
                        ctx.s_ld.inc(g.dma_start(ctx.d_dbg_pT2.ap(), A.pT[:, :, :]), 16)
                        ctx.attn_ld = ctx.s_ld.v

        with nc.Block() as block:
            pe0 = ctx.s_pe.v
            dv0 = ctx.s_dv.v

            @block.tensor
            def _(t):
                for k in range(4):
                    uwait(t, ctx, ub + 2 + k)
                for tc in range(NTC):
                    reg = psA[:, 0:1024] if tc % 2 == 0 else psA[:, 1024:2048]
                    if tc >= 2:
                        t.wait_ge(ctx.s_dv.h, dv0 + 2 * (tc - 1))
                    hb = (tc % 2) * 64
                    hcp = tc // 2
                    for j in range(16):
                        last = j == 15
                        lhsT = A.pT[hb:hb + 64, hcp, j::16]
                        ins0 = t.matmul(reg[:, 0:512], lhsT,
                                        uwo[j // 8][0][hb:hb + 64, j % 8, :],
                                        start=(j == 0), stop=last)
                        if j == 0:
                            attach_uf(ctx, ins0)
                        ins = t.matmul(reg[:, 512:1024], lhsT,
                                       uwo[j // 8][1][hb:hb + 64, j % 8, :],
                                       start=(j == 0), stop=last)
                    ctx.s_pe.inc(ins)
                    if tc == NTC - 1:
                        uf_free(ctx, 4)

            @block.vector
            def _(v):
                v.wait_ge(ctx.s_ld.h, ctx.attn_ld)
                for tc in range(NTC):
                    v.wait_ge(ctx.s_pe.h, pe0 + tc + 1)
                    reg = psA[:, 0:1024] if tc % 2 == 0 else psA[:, 1024:2048]
                    v.tensor_tensor(ctx.X[:, tc, :], ctx.X[:, tc, :], reg, ALU.add)
                    ctx.s_dv.inc(v.tensor_tensor(ctx.X[:, tc, :], ctx.X[:, tc, :],
                                                 bo_bc[:, :], ALU.add), 2)


def emit_attn_head(ctx, A, h):
    nc = ctx.nc
    hrow = (h % 2) * 64
    hc = h // 2
    psA, psB = A.psA, A.psB
    pvt = psB[:, 0:512].bitcast(BF16)
    psAbt = psA[:, 0:64].bitcast(BF16)
    with nc.Block() as block:
        pe0 = ctx.s_pe.v
        dv0 = ctx.s_dv.v
        ac0 = ctx.s_ac.v
        if h == 0:
            @block.gpsimd
            def _(g):
                ctx.s_ld.inc(g.dma_start(A.extL[64:65, :], ctx.d_onesrow.ap()), 16)
                ctx.s_ld.inc(g.dma_start(A.extR[65:66, :], ctx.d_onesrow.ap()), 16)
                emit_unit_dmas(ctx, g, ctx.units_issued, ctx.attn_ub + 6)
        ld0 = ctx.s_ld.v

        @block.tensor
        def _(t):
            for kc in range(NTC):
                ins = t.transpose(pvt[:, kc * 64:(kc + 1) * 64],
                                  A.pT[hrow:hrow + 64, hc, kc * 128:(kc + 1) * 128],
                                  ctx.ident_bf[hrow:hrow + 64, hrow:hrow + 64])
                if kc == 0:
                    attach_uf(ctx, ins)
            ctx.s_pe.inc(ins)

        @block.scalar
        def _(sc):
            sc.wait_ge(ctx.s_pe.h, pe0 + 1)
            for kc in range(NTC):
                sc.activation(ctx.d2scr[:, :], pvt[:, kc * 64:(kc + 1) * 64],
                              AF.Square, accum_out=A.d2col[:, kc:kc + 1])
            sc.mul(A.wneg[:, :], A.d2col[:, :], -0.5)
            ctx.s_ac.inc(sc.copy(A.wnegb[:, :], A.wneg[:, :]))
            for kc in range(NTC):
                ins = sc.activation(A.scolh[:, kc * 8:kc * 8 + 1], A.wnegb[:, kc:kc + 1],
                                    AF.Exp, bias=ctx.negC[:, :], scale=-0.125)
            ctx.s_ac.inc(ins)

        @block.tensor
        def _(t):
            t.wait_ge(ctx.s_ac.h, ac0 + 1)
            ctx.s_pe.inc(t.transpose(psAbt[0:16, 0:128], A.wnegb[:, :],
                                     ctx.ident_bf[:, :]))

        @block.scalar
        def _(sc):
            sc.wait_ge(ctx.s_pe.h, pe0 + 2)
            ctx.s_ac.inc(sc.copy(A.wrow_sb[0:16, :], psAbt[0:16, 0:128]))

        @block.sync
        def _(s):
            s.wait_ge(ctx.s_ac.h, ac0 + 3)
            ctx.s_ld.inc(s.dma_start(A.extL[0:64, :], A.pT[hrow:hrow + 64, hc, :]), 16)
            ctx.s_ld.inc(s.dma_start(A.extR[0:64, :], A.pT[hrow:hrow + 64, hc, :]), 16)
            ctx.s_ld.inc(s.dma_start(
                A.extL[65:66, :].rearrange("o (c p) -> o c p", p=P),
                A.wrow_sb[0:16, :]), 16)
            ctx.s_ld.inc(s.dma_start(
                A.extR[64:65, :].rearrange("o (c p) -> o c p", p=P),
                A.wrow_sb[0:16, :]), 16)

        @block.vector
        def _(v):
            v.wait_ge(ctx.s_ac.h, ac0 + 2)
            for kc in range(NTC):
                ins = v.tensor_single_scalar(A.vhead[:, kc, 0:64],
                                             pvt[:, kc * 64:(kc + 1) * 64],
                                             A.scolh[:, kc * 8:kc * 8 + 1], ALU.mult)
            ctx.s_dv.inc(ins)

        @block.scalar
        def _(sc):
            sc.wait_ge(ctx.s_dv.h, dv0 + 1)
            for kc in range(NTC):
                ins = sc.copy(A.vhead[:, kc, 64:65], A.scolh[:, kc * 8:kc * 8 + 1])
            ctx.s_ac.inc(ins)

        @block.tensor
        def _(t):
            t.wait_ge(ctx.s_ac.h, ac0 + 4)
            t.wait_ge(ctx.s_ld.h, ld0 + 64)
            for kc in range(NTC):
                for tH in range(2):
                    if 2 * kc + tH >= 2:
                        t.wait_ge(ctx.s_ac.h, ac0 + 4 + (2 * kc + tH - 1))
                    reg = psA[:, tH * 1024:(tH + 1) * 1024]
                    for ns in range(2):
                        ins = t.matmul(reg[:, ns * 512:(ns + 1) * 512],
                                       A.extL[:, kc * 128:(kc + 1) * 128],
                                       A.extR[:, tH * 1024 + ns * 512:tH * 1024 + (ns + 1) * 512],
                                       start=True, stop=True)
                    ctx.s_pe.inc(ins)
                t.wait_ge(ctx.s_ac.h, ac0 + 4 + 2 * kc + 2)
                for ns in range(4):
                    ins = t.matmul(psB[0:65, ns * 512:(ns + 1) * 512],
                                   A.vhead[:, kc, 0:65],
                                   A.ebuf[:, kc % 2, ns * 512:(ns + 1) * 512],
                                   start=(kc == 0), stop=(kc == NTC - 1))
                ctx.s_pe.inc(ins)

        @block.scalar
        def _(sc):
            for kc in range(NTC):
                for tH in range(2):
                    sc.wait_ge(ctx.s_pe.h, pe0 + 2 + 3 * kc + tH + 1)
                    if kc >= 2:
                        sc.wait_ge(ctx.s_pe.h, pe0 + 2 + 3 * (kc - 2) + 3)
                    reg = psA[:, tH * 1024:(tH + 1) * 1024]
                    ctx.s_ac.inc(sc.activation(A.ebuf[:, kc % 2, tH * 1024:(tH + 1) * 1024],
                                               reg, AF.Exp, scale=0.125,
                                               bias=ctx.zerot[:, :]))

        @block.scalar
        def _(sc):
            sc.wait_ge(ctx.s_pe.h, pe0 + 2 + 3 * NTC)
            ctx.s_ac.inc(sc.copy(A.av_sb[0:65, :], psB[0:65, :]))

        @block.sync
        def _(s):
            s.wait_ge(ctx.s_ac.h, ac0 + 37)
            ctx.s_ld.inc(s.dma_start(A.zp0[0:1, :], A.av_sb[64:65, :]), 16)

        @block.vector
        def _(v):
            v.wait_ge(ctx.s_ld.h, ld0 + 80)
            ctx.s_dv.inc(v.reciprocal(A.zp0[0:1, :], A.zp0[0:1, :]))

        @block.scalar
        def _(sc):
            sc.wait_ge(ctx.s_dv.h, dv0 + 2)
            ctx.s_ac.inc(sc.copy(A.rz1[0:1, :], A.zp0[0:1, :]))

        @block.tensor
        def _(t):
            t.wait_ge(ctx.s_ac.h, ac0 + 38)
            for ns in range(4):
                ins = t.matmul(psA[0:64, ns * 512:(ns + 1) * 512], ctx.ones1x64[:, :],
                               A.rz1[0:1, ns * 512:(ns + 1) * 512], start=True, stop=True)
            ctx.s_pe.inc(ins)

        @block.vector
        def _(v):
            v.wait_ge(ctx.s_pe.h, pe0 + 2 + 3 * NTC + 1)
            ctx.s_dv.inc(v.tensor_tensor(A.pT[hrow:hrow + 64, hc, :], A.av_sb[0:64, :],
                                         psA[0:64, :], ALU.mult))


def emit_l1prep(ctx):
    nc = ctx.nc
    ctx.wct_bf = _sb(ctx, "wct_bf", [P, NDC, D], BF16)
    ctx.bchalf = _sb(ctx, "bchalf", [P, D], F32)
    bl1sb = _sb(ctx, "bl1sb", [32, D], F32)
    s32sb = _sb(ctx, "s32sb", [32, P], F32)
    sbc128 = _sb(ctx, "sbc128", [P, 32], F32)
    with ExitStack() as ph:
        uid = ctx.uid()
        onehot = _sb(ctx, f"onehot{uid}", [8, 32], F32, stack=ph)
        idsf = _sb(ctx, f"idsf{uid}", [8, 1], F32, stack=ph)
        idsi = _sb(ctx, f"idsi{uid}", [8, 1], I32, stack=ph)
        wtsf = _sb(ctx, f"wtsf{uid}", [8, 1], F32, stack=ph)
        wts_bc = _sb(ctx, f"wtsbc{uid}", [8, P], F32, stack=ph)
        ones8 = _sb(ctx, f"ones8{uid}", [8, P], F32, stack=ph)
        iota32 = _sb(ctx, f"iota32{uid}", [8, 32], F32, stack=ph)
        sa_t = [_sb(ctx, f"sa{i}_{uid}", [P, 1], F32, stack=ph) for i in range(2)]
        wtb = [_sb(ctx, f"wtb{i}_{uid}", [P, 4, D], BF16, stack=ph) for i in range(2)]
        accW = _sb(ctx, f"accW{uid}", [P, NDC, D], F32, stack=ph)
        ps1 = _ps(ctx, f"ps1{uid}", [P, 1024], F32, stack=ph)
        ps2 = _ps(ctx, f"ps2{uid}", [P, 1024], F32, stack=ph)
        with nc.Block() as block:
            ld0 = ctx.s_ld.v
            un0 = ctx.s_un.v
            ac0 = ctx.s_ac.v
            dv0 = ctx.s_dv.v
            pe0 = ctx.s_pe.v

            @block.sync
            def _(s):
                ctx.s_ld.inc(s.dma_start(idsi[:, :], ctx.d_ids.ap()), 16)
                ctx.s_ld.inc(s.dma_start(wtsf[:, :], ctx.d_wts.ap()), 16)
                ctx.s_ld.inc(s.dma_start(iota32[:, :], ctx.d_iota32.ap()), 16)
                ctx.s_ld.inc(s.dma_start(ones8[:, :], ctx.d_ones8.ap()), 16)
                ctx.s_ld.inc(s.dma_start(bl1sb[:, :], ctx.d_bl1.ap()), 16)

            @block.scalar
            def _(sc):
                sc.wait_ge(ctx.s_ld.h, ld0 + 80)
                ctx.s_ac.inc(sc.copy(idsf[:, :], idsi[:, :]))

            @block.vector
            def _(v):
                v.wait_ge(ctx.s_ac.h, ac0 + 1)
                v.tensor_single_scalar(onehot[:, :], iota32[:, :], idsf[:, :], ALU.is_equal)
                ctx.s_dv.inc(v.tensor_single_scalar(wts_bc[:, :], ones8[:, :],
                                                    wtsf[:, :], ALU.mult))

            @block.tensor
            def _(t):
                t.wait_ge(ctx.s_dv.h, dv0 + 1)
                ins = t.matmul(ps1[0:32, 0:128], onehot[:, :], wts_bc[:, :],
                               start=True, stop=True)
                ctx.s_pe.inc(ins)

            @block.scalar
            def _(sc):
                sc.wait_ge(ctx.s_pe.h, pe0 + 1)
                ctx.s_ac.inc(sc.copy(s32sb[:, :], ps1[0:32, 0:128]))

            @block.tensor
            def _(t):
                t.wait_ge(ctx.s_ac.h, ac0 + 2)
                ctx.s_pe.inc(t.transpose(ps2[:, 0:32], s32sb[:, :],
                                         ctx.ident_f32[0:32, 0:32]))
                for ns in range(2):
                    ins = t.matmul(ps1[:, ns * 512:(ns + 1) * 512], s32sb[:, :],
                                   bl1sb[:, ns * 512:(ns + 1) * 512],
                                   start=True, stop=True)
                    if ns == 0:
                        attach_uf(ctx, ins)
                ctx.s_pe.inc(ins)

            @block.scalar
            def _(sc):
                sc.wait_ge(ctx.s_pe.h, pe0 + 3)
                sc.copy(sbc128[:, :], ps2[:, 0:32])
                ctx.s_ac.inc(sc.mul(ctx.bchalf[:, :], ps1[:, :], 0.5))

            @block.gpsimd
            def _(g):
                wl0 = ctx.s_ld.v
                ctx.wtb_ld0 = wl0
                for a in range(32):
                    for half in range(2):
                        k = 2 * a + half
                        if k >= 2:
                            g.wait_ge(ctx.s_dv.h, dv0 + 1 + 4 * (k - 1))
                        src = ctx.d_WT.ap()[a].rearrange("(c p) o -> p c o", p=P)[
                            :, half * 4:(half + 1) * 4, :]
                        ctx.s_ld.inc(g.dma_start(wtb[k % 2][:, :, :], src), 16)

            @block.scalar
            def _(sc):
                sc.wait_ge(ctx.s_ac.h, ac0 + 3)
                for a in range(32):
                    if a >= 2:
                        sc.wait_ge(ctx.s_dv.h, dv0 + 1 + 8 * (a - 1))
                    ctx.s_ac.inc(sc.copy(sa_t[a % 2][:, :], sbc128[:, a:a + 1]))

            @block.vector
            def _(v):
                for a in range(32):
                    v.wait_ge(ctx.s_ac.h, ac0 + 3 + a + 1)
                    for half in range(2):
                        k = 2 * a + half
                        v.wait_ge(ctx.s_ld.h, ctx.wtb_ld0 + 16 * (k + 1))
                        for dq in range(4):
                            dc = half * 4 + dq
                            if a == 0:
                                ins = v.tensor_single_scalar(accW[:, dc, :],
                                                             wtb[k % 2][:, dq, :],
                                                             sa_t[0][:, :], ALU.mult)
                            else:
                                ins = v.scalar_tensor_tensor(accW[:, dc, :],
                                                             wtb[k % 2][:, dq, :],
                                                             sa_t[a % 2][:, :],
                                                             accW[:, dc, :],
                                                             ALU.mult, ALU.add)
                            ctx.s_dv.inc(ins)

            @block.scalar
            def _(sc):
                sc.wait_ge(ctx.s_dv.h, dv0 + 1 + 8 * 32)
                for dc in range(NDC):
                    ins = sc.copy(ctx.wct_bf[:, dc, :], accW[:, dc, :])
                ctx.s_ac.inc(ins)


def emit_l1ffn(ctx):
    nc = ctx.nc
    with ExitStack() as ph:
        psA = _ps(ctx, f"psA{ctx.uid()}", [P, T], F32, stack=ph)
        with nc.Block() as block:
            pe0 = ctx.s_pe.v
            dv0 = ctx.s_dv.v

            @block.tensor
            def _(t):
                for tc in range(NTC):
                    reg = psA[:, 0:1024] if tc % 2 == 0 else psA[:, 1024:2048]
                    if tc >= 2:
                        t.wait_ge(ctx.s_dv.h, dv0 + 2 * (tc - 1))
                    for dc in range(NDC):
                        last = dc == NDC - 1
                        ins0 = t.matmul(reg[:, 0:512], ctx.xlnT[:, dc, tc * 128:(tc + 1) * 128],
                                        ctx.wct_bf[:, dc, 0:512], start=(dc == 0), stop=last)
                        if tc == 0 and dc == 0:
                            attach_uf(ctx, ins0)
                        ins = t.matmul(reg[:, 512:1024],
                                       ctx.xlnT[:, dc, tc * 128:(tc + 1) * 128],
                                       ctx.wct_bf[:, dc, 512:1024], start=(dc == 0),
                                       stop=last)
                    ctx.s_pe.inc(ins)

            @block.vector
            def _(v):
                for tc in range(NTC):
                    v.wait_ge(ctx.s_pe.h, pe0 + tc + 1)
                    reg = psA[:, 0:1024] if tc % 2 == 0 else psA[:, 1024:2048]
                    v.scalar_tensor_tensor(ctx.X[:, tc, :], reg, 0.5, ctx.X[:, tc, :],
                                           ALU.mult, ALU.add)
                    ctx.s_dv.inc(v.tensor_tensor(ctx.X[:, tc, :], ctx.X[:, tc, :],
                                                 ctx.bchalf[:, :], ALU.add), 2)


def emit_l1attn(ctx, xlntok):
    nc = ctx.nc
    ub = ctx.unit_base[("l1attn",)]
    uq_a, uq_b = ctx.ring[ub % NRING], ctx.ring[(ub + 1) % NRING]
    with ExitStack() as ph:
        uid = ctx.uid()
        Gtmp = _sb(ctx, f"Gtmp{uid}", [P, D], BF16, stack=ph)
        qtmp = _sb(ctx, f"qtmp{uid}", [P, T], BF16, stack=ph)
        psG = _ps(ctx, f"psG{uid}", [P, 1024], F32, stack=ph)
        psQ = _ps(ctx, f"psQ{uid}", [P, 1024], F32, stack=ph)
        psO = [_ps(ctx, f"psO{i}_{uid}", [P, 1024], F32, stack=ph) for i in range(2)]
        with nc.Block() as block:
            pe0 = ctx.s_pe.v
            dv0 = ctx.s_dv.v
            ac0 = ctx.s_ac.v

            @block.gpsimd
            def _(g):
                emit_unit_dmas(ctx, g, ctx.units_issued, ub + 2)

            @block.tensor
            def _(t):
                uwait(t, ctx, ub)
                uwait(t, ctx, ub + 1)
                for ec in range(NDC):
                    base = pe0 + ec * 19
                    if ec >= 1:
                        t.wait_ge(ctx.s_ac.h, ac0 + 3 * ec)
                    for tc in range(NTC):
                        last = tc == NTC - 1
                        ins0 = t.matmul(psG[:, 0:512], xlntok[:, tc, ec * 128:(ec + 1) * 128],
                                        xlntok[:, tc, 0:512], start=(tc == 0), stop=last)
                        if ec == 0 and tc == 0:
                            attach_uf(ctx, ins0)
                        ins = t.matmul(psG[:, 512:1024],
                                       xlntok[:, tc, ec * 128:(ec + 1) * 128],
                                       xlntok[:, tc, 512:1024], start=(tc == 0), stop=last)
                    ctx.s_pe.inc(ins)
                    usrc = uq_a if ec < 4 else uq_b
                    for th in range(2):
                        if th == 1:
                            t.wait_ge(ctx.s_ac.h, ac0 + 3 * ec + 2)
                        for dc in range(NDC):
                            last = dc == NDC - 1
                            t.matmul(psQ[:, 0:512],
                                     usrc[:, dc, (ec % 4) * 128:(ec % 4 + 1) * 128],
                                     ctx.xlnT[:, dc, th * 1024:th * 1024 + 512],
                                     start=(dc == 0), stop=last)
                            ins = t.matmul(psQ[:, 512:1024],
                                           usrc[:, dc, (ec % 4) * 128:(ec % 4 + 1) * 128],
                                           ctx.xlnT[:, dc, th * 1024 + 512:th * 1024 + 1024],
                                           start=(dc == 0), stop=last)
                        ctx.s_pe.inc(ins)
                    if ec == NDC - 1:
                        uf_free(ctx, 2)
                    t.wait_ge(ctx.s_ac.h, ac0 + 3 * ec + 3)
                    for tc in range(NTC):
                        j = ec * NTC + tc
                        if j >= 2:
                            t.wait_ge(ctx.s_dv.h, dv0 + j - 1)
                        for ns in range(2):
                            ins = t.matmul(psO[j % 2][:, ns * 512:(ns + 1) * 512],
                                           qtmp[:, tc * 128:(tc + 1) * 128],
                                           Gtmp[:, ns * 512:(ns + 1) * 512],
                                           start=True, stop=True)
                        ctx.s_pe.inc(ins)

            @block.scalar
            def _(sc):
                for ec in range(NDC):
                    base = pe0 + ec * 19
                    sc.wait_ge(ctx.s_pe.h, base + 1)
                    ctx.s_ac.inc(sc.copy(Gtmp[:, :], psG[:, :]))
                    sc.wait_ge(ctx.s_pe.h, base + 2)
                    ctx.s_ac.inc(sc.copy(qtmp[:, 0:1024], psQ[:, :]))
                    sc.wait_ge(ctx.s_pe.h, base + 3)
                    ctx.s_ac.inc(sc.copy(qtmp[:, 1024:2048], psQ[:, :]))

            @block.vector
            def _(v):
                for ec in range(NDC):
                    for tc in range(NTC):
                        j = ec * NTC + tc
                        v.wait_ge(ctx.s_pe.h, pe0 + ec * 19 + 3 + tc + 1)
                        ctx.s_dv.inc(v.tensor_tensor(ctx.X[:, tc, :], ctx.X[:, tc, :],
                                                     psO[j % 2][:, :], ALU.add))


def emit_setup(ctx):
    nc = ctx.nc
    with nc.Block() as block:
        @block.sync
        def _(s):
            ctx.s_ld.inc(s.dma_start(ctx.X[:, :, :],
                                     ctx.d_x.ap().rearrange("(c p) d -> p c d", p=P)), 16)
            ctx.s_ld.inc(s.dma_start(ctx.ident_f32[:, :], ctx.d_idf32.ap()), 16)
            ctx.s_ld.inc(s.dma_start(ctx.iota8_bc[:, :], ctx.d_iota8.ap()), 16)
            ctx.s_ld.inc(s.dma_start(ctx.ones1x64[:, :], ctx.d_ones64.ap()), 16)
            ctx.ld_setup = ctx.s_ld.v

        @block.gpsimd
        def _(g):
            ctx.s_cn.inc(g.dma_start(ctx.ident_bf[:, :], ctx.d_idf32.ap()), 16)
            ctx.s_cn.inc(g.dma_start(
                ctx.rwT_bf[:, :, :],
                ctx.d_rwT.ap().rearrange("(c p) e -> p c e", p=P)), 16)
            g.memset(ctx.negC[:, :], -STAB_C)
            g.memset(ctx.zerot[:, :], 0.0)
            g.memset(ctx.epsc[:, :], 1e-5)
            emit_unit_dmas(ctx, g, 0, min(NRING, len(ctx.units)))


def emit_store(ctx):
    nc = ctx.nc
    with nc.Block() as block:
        @block.sync
        def _(s):
            tgt = ctx.s_st.inc(s.dma_start(
                ctx.d_out.ap().rearrange("(c p) d -> p c d", p=P), ctx.X[:, :, :]), 16)
            s.wait_ge(ctx.s_st.h, tgt)


def build_program(plan=None):
    if plan is None:
        plan = FULL_PLAN
    nc = bass.Bass("TRN2", target_bir_lowering=False, debug=False)
    ctx = Ctx()
    ctx.nc = nc
    ctx._uid = 0

    def uid():
        ctx._uid += 1
        return ctx._uid
    ctx.uid = uid

    ctx.d_x = nc.dram_tensor("x_in", [T, D], F32, kind="ExternalInput")
    ctx.d_w1T = nc.dram_tensor("w1T", [NE, D, 2 * FH], F32, kind="ExternalInput")
    ctx.d_w2T = nc.dram_tensor("w2T", [NE, FH, D], F32, kind="ExternalInput")
    ctx.d_rwT = nc.dram_tensor("rwT", [D, 8], F32, kind="ExternalInput")
    ctx.d_rb_bc = nc.dram_tensor("rb_bc", [P, 8], F32, kind="ExternalInput")
    ctx.d_b1cols = nc.dram_tensor("b1cols", [P, NE, 64], F32, kind="ExternalInput")
    ctx.d_b2 = nc.dram_tensor("b2", [NE, D], F32, kind="ExternalInput")
    ctx.d_WvT = nc.dram_tensor("WvT", [D, D], F32, kind="ExternalInput")
    ctx.d_bv_cols = nc.dram_tensor("bv_cols", [P, 8], F32, kind="ExternalInput")
    ctx.d_WoTd = nc.dram_tensor("WoTd", [P, 16, D], F32, kind="ExternalInput")
    ctx.d_bo_bc = nc.dram_tensor("bo_bc", [P, D], F32, kind="ExternalInput")
    ctx.d_QT = nc.dram_tensor("QT", [D, D], F32, kind="ExternalInput")
    ctx.d_WT = nc.dram_tensor("WT_l1", [32, D, D], F32, kind="ExternalInput")
    ctx.d_bl1 = nc.dram_tensor("b_l1", [32, D], F32, kind="ExternalInput")
    ctx.d_ids = nc.dram_tensor("ids_l1", [8, 1], I32, kind="ExternalInput")
    ctx.d_wts = nc.dram_tensor("wts_l1", [8, 1], F32, kind="ExternalInput")
    ctx.d_idf32 = nc.dram_tensor("ident_f32", [P, P], F32, kind="ExternalInput")
    ctx.d_iota8 = nc.dram_tensor("iota8", [P, 8], F32, kind="ExternalInput")
    ctx.d_iota32 = nc.dram_tensor("iota32", [8, 32], F32, kind="ExternalInput")
    ctx.d_ones8 = nc.dram_tensor("ones8x128", [8, P], F32, kind="ExternalInput")
    ctx.d_ones64 = nc.dram_tensor("ones1x64", [1, 64], F32, kind="ExternalInput")
    ctx.d_onesrow = nc.dram_tensor("ones_row", [1, T], F32, kind="ExternalInput")
    ctx.d_out = nc.dram_tensor("x_out", [T, D], F32, kind="ExternalOutput")
    if DEBUG_ATTN:
        ctx.d_dbg_pT = nc.dram_tensor("dbg_pT", [P, NDC, T], F32, kind="ExternalOutput")
        ctx.d_dbg_pT2 = nc.dram_tensor("dbg_pT2", [P, NDC, T], F32, kind="ExternalOutput")
        ctx.d_dbg_d2 = nc.dram_tensor("dbg_d2", [P, NTC], F32, kind="ExternalOutput")
        ctx.d_dbg_av = nc.dram_tensor("dbg_av", [65, T], F32, kind="ExternalOutput")
        ctx.d_dbg_eb = nc.dram_tensor("dbg_eb", [P, 2, T], F32, kind="ExternalOutput")
        ctx.d_dbg_vh = nc.dram_tensor("dbg_vh", [P, NTC, 72], F32, kind="ExternalOutput")
    if DEBUG_MOE:
        ctx.d_dbg_lsb = nc.dram_tensor("dbg_lsb", [P, NTC, 8], F32, kind="ExternalOutput")
        ctx.d_dbg_mx8 = nc.dram_tensor("dbg_mx8", [P, NTC, 8], F32, kind="ExternalOutput")
        ctx.d_dbg_idx = nc.dram_tensor("dbg_idx", [P, NTC, 8], F32, kind="ExternalOutput")
        ctx.d_dbg_w8c = nc.dram_tensor("dbg_w8c", [P, NTC, 8], F32, kind="ExternalOutput")
        ctx.d_dbg_wstr = nc.dram_tensor("dbg_wstr", [P, NTC, 8, 8], F32, kind="ExternalOutput")

    ctx.units = unit_schedule(plan)
    ctx.unit_base = {}
    k = 0
    for st in plan:
        if st == "moe0":
            ctx.unit_base[("moe", 0)] = k
            k += 24 * NE
        elif st == "moe1":
            ctx.unit_base[("moe", 1)] = k
            k += 24 * NE
        elif st == "attn":
            ctx.unit_base[("attn",)] = k
            k += 6
        elif st == "l1attn":
            ctx.unit_base[("l1attn",)] = k
            k += 2
    ctx.units_issued = 0
    ctx.pending_uf = 0

    with ExitStack() as stack:
        ctx.stack = stack
        for name in ["s_ld", "s_un", "s_uf", "s_pe", "s_m1", "s_e1", "s_hr",
                     "s_m2", "s_e2", "s_ac", "s_dv", "s_st", "s_dvb", "s_cn"]:
            setattr(ctx, name, Sem(stack, nc, name))

        ctx.X = _sb(ctx, "Xres", [P, NTC, D], F32)
        ctx.xlnT = _sb(ctx, "xlnT", [P, NDC, T], BF16)
        ctx.ring = [_sb(ctx, f"ring{i}", [P, 8, 512], BF16) for i in range(NRING)]
        ctx.rwT_bf = _sb(ctx, "rwT_bf", [P, 8, 8], BF16)
        ctx.ident_bf = _sb(ctx, "identbf", [P, P], BF16)
        ctx.ident_f32 = _sb(ctx, "identf32", [P, P], F32)
        ctx.iota8_bc = _sb(ctx, "iota8bc", [P, 8], F32)
        ctx.negC = _sb(ctx, "negC", [P, 1], F32)
        ctx.zerot = _sb(ctx, "zerot", [P, 1], F32)
        ctx.epsc = _sb(ctx, "epsc", [P, 1], F32)
        ctx.ones1x64 = _sb(ctx, "ones1x64b", [1, 64], F32)
        ctx.d2scr = _sb(ctx, "d2scr", [P, 64], F32)
        ctx.s1buf = _sb(ctx, "s1buf", [P, NTC], F32)
        ctx.s2buf = _sb(ctx, "s2buf", [P, NTC], F32)
        ctx.msqbuf = _sb(ctx, "msqbuf", [P, NTC], F32)
        ctx.varbuf = _sb(ctx, "varbuf", [P, NTC], F32)
        ctx.stdbuf = _sb(ctx, "stdbuf", [P, NTC], F32)
        ctx.mstr = _sb(ctx, "mstr", [P, NTC * 8], F32)
        ctx.rstdstr = _sb(ctx, "rstdstr", [P, NTC * 8], F32)

        emit_setup(ctx)
        i = 0
        while i < len(plan):
            st = plan[i]
            if st == "ln":
                emit_ln(ctx)
            elif st == "lng":
                with ExitStack() as gsc:
                    xtok = _sb(ctx, f"xtokg{ctx.uid()}", [P, NTC, D], BF16, stack=gsc)
                    emit_ln(ctx, xlntok=xtok)
                    if i + 1 < len(plan) and plan[i + 1] == "l1attn":
                        emit_l1attn(ctx, xtok)
                        i += 1
            elif st == "moe0":
                emit_moe(ctx, 0)
            elif st == "moe1":
                emit_moe(ctx, 1)
            elif st == "attn":
                emit_attn(ctx)
            elif st == "l1prep":
                emit_l1prep(ctx)
            elif st == "l1ffn":
                emit_l1ffn(ctx)
            elif st == "l1attn":
                raise RuntimeError("l1attn must follow lng")
            i += 1
        emit_store(ctx)
    return nc


PLAN_OVERRIDE = None
DEBUG_MOE = False
DEBUG_ATTN = False
DEBUG_HEAD = 0


def kernel(L0_attn_x, L0_attn_Wv, L0_attn_bv, L0_attn_Wo, L0_attn_bo,
           L0_ffn_router_w, L0_ffn_router_b, L0_ffn_w1, L0_ffn_b1, L0_ffn_w2,
           L0_ffn_b2, L1_attn_queries, L1_ffn_W, L1_ffn_b, L1_ffn_ids, L1_ffn_weights):
    plan = PLAN_OVERRIDE if PLAN_OVERRIDE is not None else FULL_PLAN
    f32 = np.float32

    def asf(a):
        return np.ascontiguousarray(np.asarray(a), dtype=f32)

    x = asf(L0_attn_x)
    w1T = np.ascontiguousarray(asf(L0_ffn_w1).transpose(0, 2, 1))
    w2T = np.ascontiguousarray(asf(L0_ffn_w2).transpose(0, 2, 1))
    rwT = np.ascontiguousarray(asf(L0_ffn_router_w).T)
    rb_bc = np.ascontiguousarray(np.broadcast_to(asf(L0_ffn_router_b), (P, 8)))
    b1cols = np.ascontiguousarray(asf(L0_ffn_b1).reshape(NE, 64, P).transpose(2, 0, 1))
    b2 = asf(L0_ffn_b2)
    WvT = np.ascontiguousarray(asf(L0_attn_Wv).T)
    bv = asf(L0_attn_bv)
    bv_cols = np.ascontiguousarray(bv.reshape(8, P).T)
    WoT = np.ascontiguousarray(asf(L0_attn_Wo).T)
    WoTd = np.empty((P, 16, D), dtype=f32)
    for j in range(16):
        blk = WoT[j * 64:(j + 1) * 64, :]
        WoTd[0:64, j] = blk
        WoTd[64:128, j] = blk
    bo_bc = np.ascontiguousarray(np.broadcast_to(asf(L0_attn_bo), (P, D)))
    QT = np.ascontiguousarray(asf(L1_attn_queries).T)
    WT = np.ascontiguousarray(asf(L1_ffn_W).transpose(0, 2, 1))
    bl1 = asf(L1_ffn_b)
    ids = np.ascontiguousarray(np.asarray(L1_ffn_ids).astype(np.int32).reshape(8, 1))
    wts = asf(L1_ffn_weights).reshape(8, 1)
    ident_f32 = np.eye(P, dtype=f32)
    iota8 = np.ascontiguousarray(np.broadcast_to(np.arange(8, dtype=f32), (P, 8)))
    iota32 = np.ascontiguousarray(np.broadcast_to(np.arange(32, dtype=f32), (8, 32)))
    ones8 = np.ones((8, P), dtype=f32)
    ones64 = np.ones((1, 64), dtype=f32)
    onesrow = np.ones((1, T), dtype=f32)

    shared = dict(w1T=w1T, w2T=w2T, rwT=rwT, rb_bc=rb_bc, b1cols=b1cols, b2=b2,
                  WvT=WvT, bv_cols=bv_cols, WoTd=WoTd, bo_bc=bo_bc,
                  QT=QT, WT_l1=WT, b_l1=bl1, ids_l1=ids, wts_l1=wts,
                  ident_f32=ident_f32, iota8=iota8,
                  iota32=iota32, ones8x128=ones8, ones1x64=ones64, ones_row=onesrow)

    nc = build_program(plan)
    in_maps = [dict(shared, x_in=np.ascontiguousarray(x[i])) for i in range(8)]
    res = run_bass_kernel_spmd(nc, in_maps, core_ids=list(range(8)))
    return np.stack([r["x_out"] for r in res.results], axis=0)


# version 04 (final)
# speedup vs baseline: 1.0196x; 1.0196x over previous
import numpy as np
import ml_dtypes
from contextlib import ExitStack

import concourse.bass as bass
import concourse.mybir as mybir
from concourse.bass_utils import run_bass_kernel_spmd

F32 = mybir.dt.float32
BF16 = mybir.dt.bfloat16
I32 = mybir.dt.int32
U32 = mybir.dt.uint32
AF = mybir.ActivationFunctionType
ALU = mybir.AluOpType
AX = mybir.AxisListType

P = 128
D = 1024
T = 2048
NTC = 16
NDC = 8
NE = 8
FH = 4096
NRING = 4
STAB_C = 48.0

FULL_PLAN = ["ln", "moe0", "ln", "attn", "ln", "moe1",
             "l1prep", "ln", "l1ffn", "lng", "l1attn", "ln", "l1ffn"]


class Sem:
    def __init__(self, stack, nc, name):
        self.h = stack.enter_context(nc.semaphore(name))
        self.v = 0

    def inc(self, inst, n=1):
        inst.then_inc(self.h, n)
        self.v += n
        return self.v


class Ctx:
    pass


def _sb(ctx, name, shape, dt=F32, stack=None):
    return (stack or ctx.stack).enter_context(ctx.nc.sbuf_tensor(name, shape, dt))


def _ps(ctx, name, shape, dt=F32, stack=None):
    return (stack or ctx.stack).enter_context(ctx.nc.psum_tensor(name, shape, dt))


def unit_schedule(plan):
    units = []

    def moe_units(call):
        nonlocal_units = units
        for e in range(NE):
            for g in range(4):
                nonlocal_units += [("w1", call, e, g, "h1a"), ("w1", call, e, g, "h2a"),
                          ("w1", call, e, g, "h1b"), ("w1", call, e, g, "h2b"),
                          ("w2", call, e, g, "a"), ("w2", call, e, g, "b")]

    for st in plan:
        if st == "moe0":
            moe_units(0)
        elif st == "moe1":
            moe_units(1)
        elif st == "attn":
            units += [("wv", "a"), ("wv", "b"),
                      ("wod", 0, "a"), ("wod", 0, "b"),
                      ("wod", 1, "a"), ("wod", 1, "b")]
        elif st == "l1attn":
            units += [("qt", "a"), ("qt", "b")]
    return units


def unit_src_ap(ctx, u):
    if u[0] == "w1":
        _, call, e, g, part = u
        r = ctx.d_w1T.ap()[e].rearrange("(c p) f -> p c f", p=P)
        f0 = {"h1a": g * 1024, "h2a": FH + g * 1024,
              "h1b": g * 1024 + 512, "h2b": FH + g * 1024 + 512}[part]
        return r[:, :, f0:f0 + 512]
    if u[0] == "w2":
        _, call, e, g, half = u
        r = ctx.d_w2T.ap()[e].rearrange("(c p) o -> p c o", p=P)
        o0 = 0 if half == "a" else 512
        return r[:, g * 8:(g + 1) * 8, o0:o0 + 512]
    if u[0] == "wod":
        _, jhalf, half = u
        o0 = 0 if half == "a" else 512
        return ctx.d_WoTd.ap()[:, jhalf * 8:(jhalf + 1) * 8, o0:o0 + 512]
    name, half = u
    dram = {"wv": ctx.d_WvT, "qt": ctx.d_QT}[name]
    r = dram.ap().rearrange("(c p) o -> p c o", p=P)
    o0 = 0 if half == "a" else 512
    return r[:, :, o0:o0 + 512]


def emit_unit_dmas(ctx, g, lo, hi):
    for k in range(lo, hi):
        if k >= NRING:
            g.wait_ge(ctx.s_uf.h, k - NRING + 1)
        ctx.s_un.inc(g.dma_start(ctx.ring[k % NRING][:, :, :],
                                 unit_src_ap(ctx, ctx.units[k])), 16)
    ctx.units_issued = hi


def uwait(eng, ctx, k):
    eng.wait_ge(ctx.s_un.h, 16 * (k + 1))


def uf_free(ctx, n=1):
    ctx.pending_uf += n


def attach_uf(ctx, ins):
    if ctx.pending_uf:
        ins.then_inc(ctx.s_uf.h, ctx.pending_uf)
        ctx.s_uf.v += ctx.pending_uf
        ctx.pending_uf = 0


def emit_ln(ctx, xlntok=None):
    nc = ctx.nc
    with ExitStack() as ph:
        if xlntok is None:
            xlntok = _sb(ctx, f"xlntok{ctx.uid()}", [P, NTC, D], BF16, stack=ph)
        sqscr = _sb(ctx, f"sqscr{ctx.uid()}", [P, D], F32, stack=ph)
        pt_ps = _ps(ctx, f"ptps{ctx.uid()}", [P, 2, 8, 128], BF16, stack=ph)
        with nc.Block() as block:
            dv0 = ctx.s_dv.v
            ac0 = ctx.s_ac.v
            pe0 = ctx.s_pe.v

            @block.vector
            def _(v):
                v.wait_ge(ctx.s_ld.h, ctx.ld_setup)
                for tc in range(NTC):
                    ins = v.reduce_sum(ctx.s1buf[:, tc:tc + 1], ctx.X[:, tc, :], AX.X)
                ctx.s_dv.inc(ins)

            @block.scalar
            def _(sc):
                sc.wait_ge(ctx.s_ld.h, ctx.ld_setup)
                ww0 = ctx.s_ww.v
                for tc in range(NTC):
                    if tc > 0:
                        sc.wait_ge(ctx.s_ww.h, ww0 + tc)
                    ctx.s_ww.inc(sc.activation(sqscr[:, :], ctx.X[:, tc, :], AF.Square,
                                               accum_out=ctx.s2buf[:, tc:tc + 1]))
                sc.wait_ge(ctx.s_dv.h, dv0 + 1)
                for tc in range(NTC):
                    sc.mul(ctx.mstr[:, tc * 8:tc * 8 + 1], ctx.s1buf[:, tc:tc + 1], 1.0 / D)
                for tc in range(NTC):
                    ins = sc.activation(ctx.msqbuf[:, tc:tc + 1],
                                        ctx.mstr[:, tc * 8:tc * 8 + 1], AF.Square)
                ctx.s_ac.inc(ins)

            @block.vector
            def _(v):
                v.wait_ge(ctx.s_ac.h, ac0 + 1)
                for tc in range(NTC):
                    ins = v.scalar_tensor_tensor(ctx.varbuf[:, tc:tc + 1],
                                                 ctx.s2buf[:, tc:tc + 1], 1.0 / D,
                                                 ctx.msqbuf[:, tc:tc + 1],
                                                 ALU.mult, ALU.subtract)
                ctx.s_dv.inc(ins)

            @block.scalar
            def _(sc):
                sc.wait_ge(ctx.s_dv.h, dv0 + 2)
                for tc in range(NTC):
                    ins = sc.activation(ctx.stdbuf[:, tc:tc + 1], ctx.varbuf[:, tc:tc + 1],
                                        AF.Sqrt, bias=ctx.epsc[:, :], scale=1.0)
                ctx.s_ac.inc(ins)

            @block.vector
            def _(v):
                v.wait_ge(ctx.s_ac.h, ac0 + 2)
                for tc in range(NTC):
                    v.reciprocal(ctx.rstdstr[:, tc * 8:tc * 8 + 1], ctx.stdbuf[:, tc:tc + 1])
                for tc in range(NTC):
                    ctx.s_dv.inc(v.tensor_scalar(xlntok[:, tc, :], ctx.X[:, tc, :],
                                                 ctx.mstr[:, tc * 8:tc * 8 + 1],
                                                 ctx.rstdstr[:, tc * 8:tc * 8 + 1],
                                                 ALU.subtract, ALU.mult))

            @block.tensor
            def _(t):
                t.wait_ge(ctx.s_cn.h, 32)
                for tc in range(NTC):
                    t.wait_ge(ctx.s_dv.h, dv0 + 3 + tc)
                    if tc >= 2:
                        t.wait_ge(ctx.s_ac.h, ac0 + 2 + (tc - 1))
                    for dc in range(NDC):
                        ins = t.transpose(pt_ps[:, tc % 2, dc, :],
                                          xlntok[:, tc, dc * 128:(dc + 1) * 128],
                                          ctx.ident_bf[:, :])
                        if tc == 0 and dc == 0:
                            attach_uf(ctx, ins)
                    ctx.s_pe.inc(ins)

            @block.scalar
            def _(sc):
                for tc in range(NTC):
                    sc.wait_ge(ctx.s_pe.h, pe0 + tc + 1)
                    ctx.s_ac.inc(sc.copy(ctx.xlnT[:, :, tc * 128:(tc + 1) * 128],
                                         pt_ps[:, tc % 2, :, :]))


def emit_moe(ctx, call):
    nc = ctx.nc
    u0 = ctx.unit_base[("moe", call)]
    ctx.m1_i = 0
    ctx.m1_i_act = 0
    ctx.m2_i = 0
    ctx.m2_i_dve = 0
    ctx.m1_base = ctx.s_m1.v
    ctx.e1_base = ctx.s_e1.v
    ctx.hr_base = ctx.s_hr.v
    ctx.m2_base = ctx.s_m2.v
    ctx.e2_base = ctx.s_e2.v
    with ExitStack() as ph:
        uid = ctx.uid()
        rb_bc = _sb(ctx, f"rbbc{uid}", [P, 8], F32, stack=ph)
        b1cols = _sb(ctx, f"b1c{uid}", [P, NE, 64], F32, stack=ph)
        b2sb = _sb(ctx, f"b2sb{uid}", [8, D], F32, stack=ph)
        wstr = _sb(ctx, f"wstr{uid}", [P, NTC, 8, 8], F32, stack=ph)
        ctx.wstr = wstr
        sub = ExitStack()
        R = Ctx()
        R.lsb = _sb(ctx, f"lsb{uid}", [P, NTC, 8], F32, stack=sub)
        R.mx8 = _sb(ctx, f"mx8{uid}", [P, NTC, 8], F32, stack=sub)
        R.idxu = _sb(ctx, f"idxu{uid}", [P, NTC, 8], U32, stack=sub)
        R.idxf = _sb(ctx, f"idxf{uid}", [P, NTC, 8], F32, stack=sub)
        R.negv1 = _sb(ctx, f"negv1{uid}", [P, NTC], F32, stack=sub)
        R.e2 = _sb(ctx, f"e2{uid}", [P, NTC], F32, stack=sub)
        R.den = _sb(ctx, f"den{uid}", [P, NTC], F32, stack=sub)
        R.idx1s = _sb(ctx, f"idx1s{uid}", [P, NTC * 8], F32, stack=sub)
        R.recs = _sb(ctx, f"recs{uid}", [P, NTC * 8], F32, stack=sub)
        R.sw0s = _sb(ctx, f"sw0s{uid}", [P, NTC * 8], F32, stack=sub)
        R.sw1s = _sb(ctx, f"sw1s{uid}", [P, NTC * 8], F32, stack=sub)
        R.w8a = _sb(ctx, f"w8a{uid}", [P, NTC, 8], F32, stack=sub)
        R.w8b = _sb(ctx, f"w8b{uid}", [P, NTC, 8], F32, stack=sub)
        R.w8c = _sb(ctx, f"w8c{uid}", [P, NTC, 8], F32, stack=sub)
        R.wT8 = _sb(ctx, f"wT8{uid}", [8, T], F32, stack=sub)
        sub.close()
        ctx.hbuf = _sb(ctx, f"hbuf{uid}", [P, 8, T], BF16, stack=ph)
        ctx.s1sb = _sb(ctx, f"s1sb{uid}", [P, 2, T], BF16, stack=ph)
        ctx.h2sb = _sb(ctx, f"h2sb{uid}", [P, 2, T], BF16, stack=ph)
        ctx.b1cols = b1cols
        Q = [_ps(ctx, f"q{i}_{uid}", [P, 1024], F32, stack=ph) for i in range(4)]
        ctx.curQ = Q

        with nc.Block() as block:
            pe0 = ctx.s_pe.v
            dv0 = ctx.s_dv.v
            ac0 = ctx.s_ac.v
            dvb0 = ctx.s_dvb.v
            ld0 = ctx.s_ld.v

            @block.sync
            def _(s):
                ctx.s_ld.inc(s.dma_start(rb_bc[:, :], ctx.d_rb_bc.ap()), 16)
                ctx.s_ld.inc(s.dma_start(b1cols[:, :, :], ctx.d_b1cols.ap()), 16)
                ctx.s_ld.inc(s.dma_start(b2sb[:, :], ctx.d_b2.ap()), 16)
                if DEBUG_MOE and call == 0:
                    s.wait_ge(ctx.s_dv.h, dv0 + 17)
                    ctx.s_ld.inc(s.dma_start(ctx.d_dbg_lsb.ap(), R.lsb[:, :, :]), 16)
                    ctx.s_ld.inc(s.dma_start(ctx.d_dbg_mx8.ap(), R.mx8[:, :, :]), 16)
                    ctx.s_ld.inc(s.dma_start(ctx.d_dbg_idx.ap(), R.idxf[:, :, :]), 16)
                    ctx.s_ld.inc(s.dma_start(ctx.d_dbg_w8c.ap(), R.w8c[:, :, :]), 16)
                    ctx.s_ld.inc(s.dma_start(ctx.d_dbg_wstr.ap(), wstr[:, :, :, :]), 16)

            @block.gpsimd
            def _(g):
                emit_unit_dmas(ctx, g, ctx.units_issued, u0 + 24 * NE)

            @block.tensor
            def _(t):
                t.wait_ge(ctx.s_cn.h, 32)
                t.wait_ge(ctx.s_ld.h, ctx.ld_setup)
                for tc in range(NTC):
                    reg = Q[0][:, (tc % 2) * 512:(tc % 2) * 512 + 8]
                    if tc >= 2:
                        t.wait_ge(ctx.s_dv.h, dv0 + (tc - 1))
                    for dc in range(NDC):
                        ins = t.matmul(reg, ctx.xlnT[:, dc, tc * 128:(tc + 1) * 128],
                                       ctx.rwT_bf[:, dc, :], start=(dc == 0),
                                       stop=(dc == NDC - 1))
                        if dc == 0:
                            attach_uf(ctx, ins)
                    ctx.s_pe.inc(ins)
                t.wait_ge(ctx.s_dv.h, dv0 + 17)
                for tc in range(NTC):
                    dstq = Q[2] if tc < 8 else Q[3]
                    ins = t.transpose(dstq[0:8, (tc % 8) * 128:(tc % 8 + 1) * 128],
                                      R.w8c[:, tc, :], ctx.ident_f32[:, :])
                    if tc in (7, 15):
                        ctx.s_pe.inc(ins)
                t.wait_ge(ctx.s_ac.h, ac0 + 2)
                for tc in range(NTC):
                    reg = Q[tc % 2]
                    if tc >= 2:
                        t.wait_ge(ctx.s_dv.h, dv0 + 17 + (tc - 1))
                    for ns in range(2):
                        ins = t.matmul(reg[:, ns * 512:(ns + 1) * 512],
                                       R.wT8[0:8, tc * 128:(tc + 1) * 128],
                                       b2sb[0:8, ns * 512:(ns + 1) * 512],
                                       start=True, stop=True)
                    ctx.s_pe.inc(ins)
                ctx.moe_guard = (ctx.s_dv.h, dv0 + 33, ctx.s_ac.h, ac0 + 2)
                for e in range(NE):
                    emit_expert_pe(ctx, t, e, u0 + 24 * e)

            @block.vector
            def _(v):
                v.wait_ge(ctx.s_ld.h, ld0 + 48)
                for tc in range(NTC):
                    v.wait_ge(ctx.s_pe.h, pe0 + tc + 1)
                    reg = Q[0][:, (tc % 2) * 512:(tc % 2) * 512 + 8]
                    ctx.s_dv.inc(v.tensor_tensor(R.lsb[:, tc, :], reg, rb_bc[:, :],
                                                 ALU.add))
                for tc in range(NTC):
                    v.max(R.mx8[:, tc, :], R.lsb[:, tc, :])
                for tc in range(NTC):
                    ins = v.max_index(R.idxu[:, tc, :], R.mx8[:, tc, :], R.lsb[:, tc, :])
                ctx.s_dvb.inc(ins)
                v.wait_ge(ctx.s_ac.h, ac0 + 1)
                for tc in range(NTC):
                    v.tensor_scalar_add(R.den[:, tc:tc + 1], R.e2[:, tc:tc + 1], 1.0)
                for tc in range(NTC):
                    v.reciprocal(R.recs[:, tc * 8:tc * 8 + 1], R.den[:, tc:tc + 1])
                for tc in range(NTC):
                    v.tensor_scalar(R.sw0s[:, tc * 8:tc * 8 + 1],
                                    R.recs[:, tc * 8:tc * 8 + 1], 0.5, None, ALU.mult)
                for tc in range(NTC):
                    v.tensor_scalar(R.sw1s[:, tc * 8:tc * 8 + 1], R.e2[:, tc:tc + 1],
                                    R.recs[:, tc * 8:tc * 8 + 1], 0.5, ALU.mult, ALU.mult)
                for tc in range(NTC):
                    v.tensor_scalar(R.w8a[:, tc, :], ctx.iota8_bc[:, :],
                                    R.idxf[:, tc, 0:1],
                                    R.sw0s[:, tc * 8:tc * 8 + 1], ALU.is_equal, ALU.mult)
                for tc in range(NTC):
                    v.tensor_scalar(R.w8b[:, tc, :], ctx.iota8_bc[:, :],
                                    R.idx1s[:, tc * 8:tc * 8 + 1],
                                    R.sw1s[:, tc * 8:tc * 8 + 1], ALU.is_equal, ALU.mult)
                for tc in range(NTC):
                    v.tensor_tensor(wstr[:, tc, :, 0], R.w8a[:, tc, :],
                                    R.w8b[:, tc, :], ALU.add)
                for tc in range(NTC):
                    ins = v.tensor_tensor(R.w8c[:, tc, :], R.w8a[:, tc, :],
                                          R.w8b[:, tc, :], ALU.add)
                ctx.s_dv.inc(ins)
                for tc in range(NTC):
                    v.wait_ge(ctx.s_pe.h, pe0 + 19 + tc)
                    ctx.s_dv.inc(v.tensor_tensor(ctx.X[:, tc, :], ctx.X[:, tc, :],
                                                 Q[tc % 2][:, :], ALU.add))
                for e in range(NE):
                    emit_expert_dve(ctx, v, e)

            @block.scalar
            def _(sc):
                sc.wait_ge(ctx.s_ld.h, ld0 + 48)
                sc.wait_ge(ctx.s_dvb.h, dvb0 + 1)
                for tc in range(NTC):
                    sc.mul(R.negv1[:, tc:tc + 1], R.mx8[:, tc, 0:1], -1.0)
                for tc in range(NTC):
                    sc.copy(R.idxf[:, tc, :], R.idxu[:, tc, :])
                for tc in range(NTC):
                    sc.activation(R.e2[:, tc:tc + 1], R.mx8[:, tc, 1:2], AF.Exp,
                                  bias=R.negv1[:, tc:tc + 1])
                for tc in range(NTC):
                    ins = sc.copy(R.idx1s[:, tc * 8:tc * 8 + 1], R.idxf[:, tc, 1:2])
                ctx.s_ac.inc(ins)
                sc.wait_ge(ctx.s_pe.h, pe0 + 18)
                sc.copy(R.wT8[0:8, 0:1024], Q[2][0:8, :])
                ctx.s_ac.inc(sc.copy(R.wT8[0:8, 1024:2048], Q[3][0:8, :]))
                for e in range(NE):
                    emit_expert_act(ctx, sc, e)


def emit_expert_pe(ctx, t, e, ubase):
    Q = ctx.curQ
    for g in range(4):
        ub = ubase + 6 * g
        for c in range(8):
            for part in range(2):
                i = ctx.m1_i
                uk = ub + (0 if c < 4 else 2) + part
                uwait(t, ctx, uk)
                p0 = Q[(i % 2) * 2]
                p1 = Q[(i % 2) * 2 + 1]
                if i == 0:
                    t.wait_ge(ctx.moe_guard[0], ctx.moe_guard[1])
                elif i == 1:
                    t.wait_ge(ctx.moe_guard[2], ctx.moe_guard[3])
                else:
                    t.wait_ge(ctx.s_e1.h, ctx.e1_base + 2 * (i - 1))
                u = ctx.ring[uk % NRING]
                fcol = (c % 4) * 128
                for dc in range(NDC):
                    last = dc == NDC - 1
                    ins0 = t.matmul(p0[:, 0:512], u[:, dc, fcol:fcol + 128],
                                    ctx.xlnT[:, dc, 0:512], start=(dc == 0), stop=last)
                    if dc == 0:
                        attach_uf(ctx, ins0)
                    t.matmul(p0[:, 512:1024], u[:, dc, fcol:fcol + 128],
                             ctx.xlnT[:, dc, 512:1024], start=(dc == 0), stop=last)
                    t.matmul(p1[:, 0:512], u[:, dc, fcol:fcol + 128],
                             ctx.xlnT[:, dc, 1024:1536], start=(dc == 0), stop=last)
                    ins = t.matmul(p1[:, 512:1024], u[:, dc, fcol:fcol + 128],
                                   ctx.xlnT[:, dc, 1536:2048], start=(dc == 0), stop=last)
                ctx.s_m1.inc(ins)
                if (c, part) in ((3, 0), (3, 1), (7, 0), (7, 1)):
                    uf_free(ctx)
                ctx.m1_i += 1
        uwait(t, ctx, ub + 4)
        uwait(t, ctx, ub + 5)
        t.wait_ge(ctx.s_hr.h, ctx.hr_base + 8 * (4 * e + g + 1))
        ua = ctx.ring[(ub + 4) % NRING]
        ubf = ctx.ring[(ub + 5) % NRING]
        for tc in range(NTC):
            j = ctx.m2_i
            if j >= 2:
                t.wait_ge(ctx.s_e2.h, ctx.e2_base + (j - 1))
            for c in range(8):
                last = c == 7
                ins0 = t.matmul(Q[j % 2][:, 0:512], ctx.hbuf[:, c, tc * 128:(tc + 1) * 128],
                                ua[:, c, :], start=(c == 0), stop=last)
                if c == 0:
                    attach_uf(ctx, ins0)
                ins = t.matmul(Q[j % 2][:, 512:1024],
                               ctx.hbuf[:, c, tc * 128:(tc + 1) * 128],
                               ubf[:, c, :], start=(c == 0), stop=last)
            ctx.s_m2.inc(ins)
            if tc == NTC - 1:
                uf_free(ctx, 2)
            ctx.m2_i += 1
        t.wait_ge(ctx.s_e2.h, ctx.e2_base + ctx.m2_i)


def emit_expert_act(ctx, sc, e):
    Q = ctx.curQ
    for g in range(4):
        for c in range(8):
            for part in range(2):
                i = ctx.m1_i_act
                sc.wait_ge(ctx.s_m1.h, ctx.m1_base + i + 1)
                m = 8 * (4 * e + g) + c
                if m >= 2 and part == 0:
                    sc.wait_ge(ctx.s_hr.h, ctx.hr_base + m - 1)
                p0 = Q[(i % 2) * 2]
                p1 = Q[(i % 2) * 2 + 1]
                fcglob = g * 8 + c + (32 if part == 1 else 0)
                bias = ctx.b1cols[:, e, fcglob:fcglob + 1]
                dst = ctx.s1sb if part == 0 else ctx.h2sb
                fn = AF.Silu if part == 0 else AF.Identity
                i1 = sc.activation(dst[:, c % 2, 0:1024], p0[:, :], fn, bias=bias)
                i2 = sc.activation(dst[:, c % 2, 1024:2048], p1[:, :], fn, bias=bias)
                ctx.s_e1.inc(i1)
                ctx.s_e1.inc(i2)
                ctx.m1_i_act += 1


def emit_expert_dve(ctx, v, e):
    Q = ctx.curQ
    for g in range(4):
        for c in range(8):
            v.wait_ge(ctx.s_e1.h, ctx.e1_base + 32 * (4 * e + g) + 4 * c + 4)
            ctx.s_hr.inc(v.tensor_tensor(ctx.hbuf[:, c, :], ctx.s1sb[:, c % 2, :],
                                         ctx.h2sb[:, c % 2, :], ALU.mult))
        for tc in range(NTC):
            j = ctx.m2_i_dve
            v.wait_ge(ctx.s_m2.h, ctx.m2_base + j + 1)
            ctx.s_e2.inc(v.scalar_tensor_tensor(
                ctx.X[:, tc, :], Q[j % 2][:, :], ctx.wstr[:, tc, e, 0:1],
                ctx.X[:, tc, :], ALU.mult, ALU.add))
            ctx.m2_i_dve += 1


def emit_attn(ctx):
    nc = ctx.nc
    ub = ctx.unit_base[("attn",)]
    ctx.attn_ub = ub
    uwv_a, uwv_b = ctx.ring[ub % NRING], ctx.ring[(ub + 1) % NRING]
    uwo = [[ctx.ring[(ub + 2 + 2 * jh + oh) % NRING] for oh in range(2)] for jh in range(2)]
    with ExitStack() as ph:
        uid = ctx.uid()
        A = Ctx()
        A.pT = _sb(ctx, f"pT{uid}", [P, NDC, T], BF16, stack=ph)
        A.ebuf = _sb(ctx, f"ebuf{uid}", [P, 2, T], BF16, stack=ph)
        A.vhead2 = _sb(ctx, f"vh2{uid}", [P, 2, NTC, 72], BF16, stack=ph)
        A.ones_bf = _sb(ctx, f"onesb{uid}", [P, 64], BF16, stack=ph)
        A.extL = [ctx.xlnT[0:65, 0, :], ctx.xlnT[0:65, 2, :]]
        A.extR = [ctx.xlnT[0:65, 1, :], ctx.xlnT[0:65, 3, :]]
        A.av_sb = ctx.xlnT[:, 4:6, :].rearrange("p a b -> p (a b)").bitcast(F32)
        A.zpA = ctx.xlnT[0:1, 7, 0:1024]
        A.zpB = ctx.xlnT[64:65, 7, 0:1024]
        bv_cols = _sb(ctx, f"bvc{uid}", [P, 8], F32, stack=ph)
        bo_bc = _sb(ctx, f"bobc{uid}", [P, D], F32, stack=ph)
        A.psA = _ps(ctx, f"psA{uid}", [P, T], F32, stack=ph)
        A.psB = _ps(ctx, f"psB{uid}", [P, T], F32, stack=ph)
        psA, psB = A.psA, A.psB

        with nc.Block() as block:
            pe0 = ctx.s_pe.v
            ac0 = ctx.s_ac.v
            ld0 = ctx.s_ld.v

            @block.sync
            def _(s):
                ctx.s_ld.inc(s.dma_start(bv_cols[:, :], ctx.d_bv_cols.ap()), 16)
                ctx.s_ld.inc(s.dma_start(bo_bc[:, :], ctx.d_bo_bc.ap()), 16)
                ctx.attn_ld = ctx.s_ld.v

            @block.gpsimd
            def _(g):
                emit_unit_dmas(ctx, g, ctx.units_issued, ub + 4)

            @block.tensor
            def _(t):
                uwait(t, ctx, ub + 0)
                uwait(t, ctx, ub + 1)
                for ec in range(NDC):
                    reg = psA if ec % 2 == 0 else psB
                    if ec >= 2:
                        t.wait_ge(ctx.s_ac.h, ac0 + (ec - 1))
                    usrc = uwv_a if ec < 4 else uwv_b
                    for dc in range(NDC):
                        last = dc == NDC - 1
                        for ns in range(4):
                            ins = t.matmul(reg[:, ns * 512:(ns + 1) * 512],
                                           usrc[:, dc, (ec % 4) * 128:(ec % 4 + 1) * 128],
                                           ctx.xlnT[:, dc, ns * 512:(ns + 1) * 512],
                                           start=(dc == 0), stop=last)
                            if dc == 0 and ns == 0:
                                attach_uf(ctx, ins)
                    ctx.s_pe.inc(ins)
                    if ec == NDC - 1:
                        uf_free(ctx, 2)

            @block.scalar
            def _(sc):
                sc.wait_ge(ctx.s_ld.h, ld0 + 16)
                for ec in range(NDC):
                    sc.wait_ge(ctx.s_pe.h, pe0 + ec + 1)
                    reg = psA if ec % 2 == 0 else psB
                    ctx.s_ac.inc(sc.activation(A.pT[:, ec, :], reg[:, :], AF.Identity,
                                               bias=bv_cols[:, ec:ec + 1]))

            if DEBUG_ATTN:
                @block.gpsimd
                def _(g):
                    g.wait_ge(ctx.s_ac.h, ac0 + 8)
                    ctx.s_ld.inc(g.dma_start(ctx.d_dbg_pT.ap(), A.pT[:, :, :]), 16)
                    ctx.attn_ld = ctx.s_ld.v

        emit_attn_heads(ctx, A)

        with nc.Block() as block:
            pe0 = ctx.s_pe.v
            dv0 = ctx.s_dv.v

            @block.tensor
            def _(t):
                for k in range(4):
                    uwait(t, ctx, ub + 2 + k)
                for tc in range(NTC):
                    reg = psA[:, 0:1024] if tc % 2 == 0 else psA[:, 1024:2048]
                    if tc >= 2:
                        t.wait_ge(ctx.s_dv.h, dv0 + 2 * (tc - 1))
                    hb = (tc % 2) * 64
                    hcp = tc // 2
                    for j in range(16):
                        last = j == 15
                        lhsT = A.pT[hb:hb + 64, hcp, j::16]
                        ins0 = t.matmul(reg[:, 0:512], lhsT,
                                        uwo[j // 8][0][hb:hb + 64, j % 8, :],
                                        start=(j == 0), stop=last)
                        if j == 0:
                            attach_uf(ctx, ins0)
                        ins = t.matmul(reg[:, 512:1024], lhsT,
                                       uwo[j // 8][1][hb:hb + 64, j % 8, :],
                                       start=(j == 0), stop=last)
                    ctx.s_pe.inc(ins)
                    if tc == NTC - 1:
                        uf_free(ctx, 4)

            @block.vector
            def _(v):
                v.wait_ge(ctx.s_ld.h, ctx.attn_ld)
                for tc in range(NTC):
                    v.wait_ge(ctx.s_pe.h, pe0 + tc + 1)
                    reg = psA[:, 0:1024] if tc % 2 == 0 else psA[:, 1024:2048]
                    v.tensor_tensor(ctx.X[:, tc, :], ctx.X[:, tc, :], reg, ALU.add)
                    ctx.s_dv.inc(v.tensor_tensor(ctx.X[:, tc, :], ctx.X[:, tc, :],
                                                 bo_bc[:, :], ALU.add), 2)


def emit_attn_heads(ctx, A):
    nc = ctx.nc
    NH = 16
    psA, psB = A.psA, A.psB
    pvt = psA[:, 1536:2048].bitcast(BF16)
    mxs = ctx.d2scr[0:1, 0:1]
    negmx = ctx.d2scr[0:1, 8:9]

    pe_ops, ac_ops, dv_ops, ld_ops = [], [], [], []
    ac_ops += [("sq", 0), ("sq", 1), ("negmx", 0), ("wrow", 0)]
    pe_ops += [("d2row", 0)]
    dv_ops += [("mx", 0)]
    ld_ops += [("extL", 0), ("extR", 0), ("extL", 1), ("extR", 1)]
    for g in range(NH):
        if g + 2 < NH:
            ac_ops.append(("sq", g + 2))
        if g >= 1:
            ac_ops.append(("bounce", g - 1))
        for t in range(8):
            ac_ops.append(("e", g, t))
        if g + 1 < NH:
            ac_ops += [("negmx", g + 1), ("wrow", g + 1)]
        for t in range(8, 64):
            ac_ops.append(("e", g, t))
        ac_ops.append(("avevac", g))

        for t in range(4):
            pe_ops.append(("t", g, t))
        if g + 1 < NH:
            pe_ops.append(("d2row", g + 1))
        for t in range(4, 8):
            pe_ops.append(("t", g, t))
        if g >= 1:
            pe_ops.append(("zb", g - 1))
        pe_ops.append(("vtrans", g))
        pe_ops.append(("av", g, 0))
        for k in range(1, 15):
            for t in range(4 * k + 4, 4 * k + 8):
                pe_ops.append(("t", g, t))
            pe_ops.append(("av", g, k))
        pe_ops.append(("av", g, 15))

        if g >= 1:
            dv_ops.append(("recip", g - 1))
        if g + 1 < NH:
            dv_ops.append(("mx", g + 1))
        if g >= 1:
            dv_ops.append(("fm", g - 1))
        for k in range(16):
            dv_ops.append(("vh", g, k))
        if g + 2 < NH:
            ld_ops += [("extL", g + 2), ("extR", g + 2)]
    pe_ops.append(("zb", NH - 1))
    ac_ops.append(("bounce", NH - 1))
    dv_ops += [("recip", NH - 1), ("fm", NH - 1)]

    evt = {}
    for ops, base, step in ((pe_ops, ctx.s_pe.v, 1), (ac_ops, ctx.s_ac.v, 1),
                            (dv_ops, ctx.s_dv.v, 1), (ld_ops, ctx.s_ld.v, 16)):
        c = base
        for op in ops:
            if op[0] != "negmx":
                c += step
            evt[op] = c
    cn_ready = ctx.s_cn.v + 32
    ww_ready = ctx.s_ww.v + 33
    global ATTN_EVT
    ATTN_EVT = dict(evt=evt, pe=pe_ops, ac=ac_ops, dv=dv_ops, ld=ld_ops,
                    base=dict(pe=ctx.s_pe.v, ac=ctx.s_ac.v, dv=ctx.s_dv.v, ld=ctx.s_ld.v))

    def hgeo(g):
        return (g % 2) * 64, g // 2

    with nc.Block() as block:
        @block.gpsimd
        def _(g_):
            ctx.s_ww.inc(g_.memset(A.ones_bf[:, :], 1.0))
            for pi in range(2):
                for kc in range(NTC):
                    ctx.s_ww.inc(g_.memset(A.vhead2[:, pi, kc, 64:65], 1.0))
            ctx.s_cn.inc(g_.dma_start(ctx.xlnT[64:65, 0, :], ctx.d_onesrow.ap()), 16)
            ctx.s_cn.inc(g_.dma_start(ctx.xlnT[64:65, 2, :], ctx.d_onesrow.ap()), 16)
            emit_unit_dmas(ctx, g_, ctx.units_issued, ctx.attn_ub + 6)

        @block.sync
        def _(s):
            for op in ld_ops:
                kind, g = op
                hr, hc = hgeo(g)
                pi = g % 2
                if kind == "extL":
                    if g >= 2:
                        s.wait_ge(ctx.s_pe.h, evt[("t", g - 2, 63)])
                    ctx.s_ld.inc(s.dma_start(A.extL[pi][0:64, :],
                                             A.pT[hr:hr + 64, hc, :]), 16)
                else:
                    ctx.s_ld.inc(s.dma_start(A.extR[pi][0:64, :],
                                             A.pT[hr:hr + 64, hc, :]), 16)
            assert ctx.s_ld.v == evt[ld_ops[-1]]

        @block.tensor
        def _(t_):
            first = True
            for op in pe_ops:
                kind, g = op[0], op[1]
                hr, hc = hgeo(g)
                pi = g % 2
                sqap = ctx.xlnT[hr:hr + 64, 6, :]
                if kind == "d2row":
                    t_.wait_ge(ctx.s_ac.h, evt[("sq", g)])
                    if g == 0:
                        t_.wait_ge(ctx.s_ww.h, ww_ready)
                    elif g == 1:
                        t_.wait_ge(ctx.s_ac.h, evt[("wrow", 0)])
                    else:
                        t_.wait_ge(ctx.s_ac.h, evt[("avevac", g - 2)])
                    for ns in range(4):
                        ins = t_.matmul(psB[0:1, ns * 512:(ns + 1) * 512],
                                        A.ones_bf[hr:hr + 64, 0:1],
                                        sqap[:, ns * 512:(ns + 1) * 512],
                                        start=True, stop=True)
                        if first:
                            attach_uf(ctx, ins)
                            first = False
                    ctx.s_pe.inc(ins)
                elif kind == "t":
                    tau = op[2]
                    if tau == 0:
                        t_.wait_ge(ctx.s_ld.h, evt[("extR", g)])
                        t_.wait_ge(ctx.s_ac.h, evt[("wrow", g)])
                        if g == 0:
                            t_.wait_ge(ctx.s_cn.h, cn_ready)
                        else:
                            t_.wait_ge(ctx.s_ac.h, evt[("e", g - 1, 63)])
                    elif tau >= 3:
                        t_.wait_ge(ctx.s_ac.h, evt[("e", g, tau - 3)])
                    kc, ns = tau // 4, tau % 4
                    ctx.s_pe.inc(t_.matmul(
                        psA[:, (tau % 3) * 512:(tau % 3 + 1) * 512],
                        A.extL[pi][:, kc * 128:(kc + 1) * 128],
                        A.extR[pi][:, ns * 512:(ns + 1) * 512],
                        start=True, stop=True))
                elif kind == "zb":
                    t_.wait_ge(ctx.s_ac.h, evt[("bounce", g)])
                    if g + 2 < NH:
                        t_.wait_ge(ctx.s_dv.h, evt[("mx", g + 2)])
                        t_.wait_ge(ctx.s_ac.h, evt[("wrow", g + 2)])
                    for ns in range(4):
                        zsrc = A.zpA if ns < 2 else A.zpB
                        zb0 = 0 if ns < 2 else 64
                        ins = t_.matmul(psB[64:128, ns * 512:(ns + 1) * 512],
                                        A.ones_bf[zb0:zb0 + 1, :],
                                        zsrc[0:1, (ns % 2) * 512:(ns % 2 + 1) * 512],
                                        start=True, stop=True)
                    ctx.s_pe.inc(ins)
                elif kind == "vtrans":
                    if g >= 1:
                        t_.wait_ge(ctx.s_dv.h, evt[("vh", g - 1, 15)])
                    for kc in range(NTC):
                        ins = t_.transpose(pvt[:, kc * 64:(kc + 1) * 64],
                                           A.pT[hr:hr + 64, hc, kc * 128:(kc + 1) * 128],
                                           ctx.ident_bf[hr:hr + 64, hr:hr + 64])
                    ctx.s_pe.inc(ins)
                else:
                    k = op[2]
                    t_.wait_ge(ctx.s_ac.h, evt[("e", g, 4 * k + 3)])
                    t_.wait_ge(ctx.s_dv.h, evt[("vh", g, k)])
                    if k == 0:
                        if g >= 1:
                            t_.wait_ge(ctx.s_ac.h, evt[("avevac", g - 1)])
                        if g + 1 < NH:
                            t_.wait_ge(ctx.s_ac.h, evt[("wrow", g + 1)])
                    for ns in range(4):
                        ins = t_.matmul(psB[0:65, ns * 512:(ns + 1) * 512],
                                        A.vhead2[:, pi, k, 0:65],
                                        A.ebuf[:, k % 2, ns * 512:(ns + 1) * 512],
                                        start=(k == 0), stop=(k == 15))
                    ctx.s_pe.inc(ins)
            assert ctx.s_pe.v == evt[pe_ops[-1]]

        @block.scalar
        def _(sc):
            for op in ac_ops:
                kind, g = op[0], op[1]
                hr, hc = hgeo(g)
                pi = g % 2
                if kind == "sq":
                    if g >= 2:
                        sc.wait_ge(ctx.s_pe.h, evt[("d2row", g - 2)])
                    ctx.s_ac.inc(sc.activation(ctx.xlnT[hr:hr + 64, 6, :],
                                               A.pT[hr:hr + 64, hc, :], AF.Square))
                elif kind == "negmx":
                    sc.wait_ge(ctx.s_dv.h, evt[("mx", g)])
                    sc.mul(negmx, mxs, -0.5)
                elif kind == "wrow":
                    if g >= 2:
                        sc.wait_ge(ctx.s_pe.h, evt[("t", g - 2, 63)])
                    ctx.s_ac.inc(sc.activation(ctx.xlnT[64:65, 2 * pi + 1, :],
                                               psB[0:1, :], AF.Identity,
                                               bias=negmx, scale=-0.5))
                elif kind == "bounce":
                    sc.wait_ge(ctx.s_dv.h, evt[("recip", g)])
                    if g >= 1:
                        sc.wait_ge(ctx.s_pe.h, evt[("zb", g - 1)])
                    sc.copy(A.zpA[0:1, :], A.av_sb[64:65, 0:1024])
                    ctx.s_ac.inc(sc.copy(A.zpB[0:1, :], A.av_sb[64:65, 1024:2048]))
                elif kind == "e":
                    tau = op[2]
                    sc.wait_ge(ctx.s_pe.h, evt[("t", g, tau)])
                    kc = tau // 4
                    if kc >= 2:
                        sc.wait_ge(ctx.s_pe.h, evt[("av", g, kc - 2)])
                    elif g >= 1:
                        sc.wait_ge(ctx.s_pe.h, evt[("av", g - 1, 14 + kc)])
                    ctx.s_ac.inc(sc.activation(
                        A.ebuf[:, kc % 2, (tau % 4) * 512:(tau % 4 + 1) * 512],
                        psA[:, (tau % 3) * 512:(tau % 3 + 1) * 512],
                        AF.Exp, scale=0.125, bias=ctx.zerot[:, :]))
                else:
                    sc.wait_ge(ctx.s_pe.h, evt[("av", g, 15)])
                    if g >= 1:
                        sc.wait_ge(ctx.s_dv.h, evt[("fm", g - 1)])
                    ctx.s_ac.inc(sc.copy(A.av_sb[0:65, :], psB[0:65, :]))
            assert ctx.s_ac.v == evt[ac_ops[-1]]

        @block.vector
        def _(v):
            for op in dv_ops:
                kind, g = op[0], op[1]
                hr, hc = hgeo(g)
                pi = g % 2
                if kind == "mx":
                    v.wait_ge(ctx.s_pe.h, evt[("d2row", g)])
                    if g >= 1:
                        v.wait_ge(ctx.s_ac.h, evt[("wrow", g - 1)])
                    ctx.s_dv.inc(v.reduce_max(mxs, psB[0:1, :], AX.X))
                elif kind == "recip":
                    v.wait_ge(ctx.s_ac.h, evt[("avevac", g)])
                    ctx.s_dv.inc(v.reciprocal(A.av_sb[64:65, :], A.av_sb[64:65, :]))
                elif kind == "fm":
                    v.wait_ge(ctx.s_pe.h, evt[("zb", g)])
                    ctx.s_dv.inc(v.tensor_tensor(A.pT[hr:hr + 64, hc, :],
                                                 A.av_sb[0:64, :],
                                                 psB[64:128, :], ALU.mult))
                else:
                    k = op[2]
                    if k == 0:
                        v.wait_ge(ctx.s_pe.h, evt[("vtrans", g)])
                    ctx.s_dv.inc(v.tensor_copy(A.vhead2[:, pi, k, 0:64],
                                               pvt[:, k * 64:(k + 1) * 64]))
            assert ctx.s_dv.v == evt[dv_ops[-1]]


def emit_l1prep(ctx):
    nc = ctx.nc
    ctx.wct_bf = _sb(ctx, "wct_bf", [P, NDC, D], BF16)
    ctx.bchalf = _sb(ctx, "bchalf", [P, D], F32)
    bl1sb = _sb(ctx, "bl1sb", [32, D], F32)
    s32sb = _sb(ctx, "s32sb", [32, P], F32)
    sbc128 = _sb(ctx, "sbc128", [P, 32], F32)
    with ExitStack() as ph:
        uid = ctx.uid()
        onehot = _sb(ctx, f"onehot{uid}", [8, 32], F32, stack=ph)
        idsf = _sb(ctx, f"idsf{uid}", [8, 1], F32, stack=ph)
        idsi = _sb(ctx, f"idsi{uid}", [8, 1], I32, stack=ph)
        wtsf = _sb(ctx, f"wtsf{uid}", [8, 1], F32, stack=ph)
        wts_bc = _sb(ctx, f"wtsbc{uid}", [8, P], F32, stack=ph)
        ones8 = _sb(ctx, f"ones8{uid}", [8, P], F32, stack=ph)
        iota32 = _sb(ctx, f"iota32{uid}", [8, 32], F32, stack=ph)
        sa_t = [_sb(ctx, f"sa{i}_{uid}", [P, 1], F32, stack=ph) for i in range(2)]
        wtb = [_sb(ctx, f"wtb{i}_{uid}", [P, 4, D], BF16, stack=ph) for i in range(2)]
        accW = _sb(ctx, f"accW{uid}", [P, NDC, D], F32, stack=ph)
        ps1 = _ps(ctx, f"ps1{uid}", [P, 1024], F32, stack=ph)
        ps2 = _ps(ctx, f"ps2{uid}", [P, 1024], F32, stack=ph)
        with nc.Block() as block:
            ld0 = ctx.s_ld.v
            un0 = ctx.s_un.v
            ac0 = ctx.s_ac.v
            dv0 = ctx.s_dv.v
            pe0 = ctx.s_pe.v

            @block.sync
            def _(s):
                ctx.s_ld.inc(s.dma_start(idsi[:, :], ctx.d_ids.ap()), 16)
                ctx.s_ld.inc(s.dma_start(wtsf[:, :], ctx.d_wts.ap()), 16)
                ctx.s_ld.inc(s.dma_start(iota32[:, :], ctx.d_iota32.ap()), 16)
                ctx.s_ld.inc(s.dma_start(ones8[:, :], ctx.d_ones8.ap()), 16)
                ctx.s_ld.inc(s.dma_start(bl1sb[:, :], ctx.d_bl1.ap()), 16)

            @block.scalar
            def _(sc):
                sc.wait_ge(ctx.s_ld.h, ld0 + 80)
                ctx.s_ac.inc(sc.copy(idsf[:, :], idsi[:, :]))

            @block.vector
            def _(v):
                v.wait_ge(ctx.s_ac.h, ac0 + 1)
                v.tensor_single_scalar(onehot[:, :], iota32[:, :], idsf[:, :], ALU.is_equal)
                ctx.s_dv.inc(v.tensor_single_scalar(wts_bc[:, :], ones8[:, :],
                                                    wtsf[:, :], ALU.mult))

            @block.tensor
            def _(t):
                t.wait_ge(ctx.s_dv.h, dv0 + 1)
                ins = t.matmul(ps1[0:32, 0:128], onehot[:, :], wts_bc[:, :],
                               start=True, stop=True)
                ctx.s_pe.inc(ins)

            @block.scalar
            def _(sc):
                sc.wait_ge(ctx.s_pe.h, pe0 + 1)
                ctx.s_ac.inc(sc.copy(s32sb[:, :], ps1[0:32, 0:128]))

            @block.tensor
            def _(t):
                t.wait_ge(ctx.s_ac.h, ac0 + 2)
                ctx.s_pe.inc(t.transpose(ps2[:, 0:32], s32sb[:, :],
                                         ctx.ident_f32[0:32, 0:32]))
                for ns in range(2):
                    ins = t.matmul(ps1[:, ns * 512:(ns + 1) * 512], s32sb[:, :],
                                   bl1sb[:, ns * 512:(ns + 1) * 512],
                                   start=True, stop=True)
                    if ns == 0:
                        attach_uf(ctx, ins)
                ctx.s_pe.inc(ins)

            @block.scalar
            def _(sc):
                sc.wait_ge(ctx.s_pe.h, pe0 + 3)
                sc.copy(sbc128[:, :], ps2[:, 0:32])
                ctx.s_ac.inc(sc.mul(ctx.bchalf[:, :], ps1[:, :], 0.5))

            @block.gpsimd
            def _(g):
                wl0 = ctx.s_ld.v
                ctx.wtb_ld0 = wl0
                for a in range(32):
                    for half in range(2):
                        k = 2 * a + half
                        if k >= 2:
                            g.wait_ge(ctx.s_dv.h, dv0 + 1 + 4 * (k - 1))
                        src = ctx.d_WT.ap()[a].rearrange("(c p) o -> p c o", p=P)[
                            :, half * 4:(half + 1) * 4, :]
                        ctx.s_ld.inc(g.dma_start(wtb[k % 2][:, :, :], src), 16)

            @block.scalar
            def _(sc):
                sc.wait_ge(ctx.s_ac.h, ac0 + 3)
                for a in range(32):
                    if a >= 2:
                        sc.wait_ge(ctx.s_dv.h, dv0 + 1 + 8 * (a - 1))
                    ctx.s_ac.inc(sc.copy(sa_t[a % 2][:, :], sbc128[:, a:a + 1]))

            @block.vector
            def _(v):
                for a in range(32):
                    v.wait_ge(ctx.s_ac.h, ac0 + 3 + a + 1)
                    for half in range(2):
                        k = 2 * a + half
                        v.wait_ge(ctx.s_ld.h, ctx.wtb_ld0 + 16 * (k + 1))
                        for dq in range(4):
                            dc = half * 4 + dq
                            if a == 0:
                                ins = v.tensor_single_scalar(accW[:, dc, :],
                                                             wtb[k % 2][:, dq, :],
                                                             sa_t[0][:, :], ALU.mult)
                            else:
                                ins = v.scalar_tensor_tensor(accW[:, dc, :],
                                                             wtb[k % 2][:, dq, :],
                                                             sa_t[a % 2][:, :],
                                                             accW[:, dc, :],
                                                             ALU.mult, ALU.add)
                            ctx.s_dv.inc(ins)

            @block.scalar
            def _(sc):
                sc.wait_ge(ctx.s_dv.h, dv0 + 1 + 8 * 32)
                for dc in range(NDC):
                    ins = sc.copy(ctx.wct_bf[:, dc, :], accW[:, dc, :])
                ctx.s_ac.inc(ins)


def emit_l1ffn(ctx):
    nc = ctx.nc
    with ExitStack() as ph:
        psA = _ps(ctx, f"psA{ctx.uid()}", [P, T], F32, stack=ph)
        with nc.Block() as block:
            pe0 = ctx.s_pe.v
            dv0 = ctx.s_dv.v

            @block.tensor
            def _(t):
                for tc in range(NTC):
                    reg = psA[:, 0:1024] if tc % 2 == 0 else psA[:, 1024:2048]
                    if tc >= 2:
                        t.wait_ge(ctx.s_dv.h, dv0 + 2 * (tc - 1))
                    for dc in range(NDC):
                        last = dc == NDC - 1
                        ins0 = t.matmul(reg[:, 0:512], ctx.xlnT[:, dc, tc * 128:(tc + 1) * 128],
                                        ctx.wct_bf[:, dc, 0:512], start=(dc == 0), stop=last)
                        if tc == 0 and dc == 0:
                            attach_uf(ctx, ins0)
                        ins = t.matmul(reg[:, 512:1024],
                                       ctx.xlnT[:, dc, tc * 128:(tc + 1) * 128],
                                       ctx.wct_bf[:, dc, 512:1024], start=(dc == 0),
                                       stop=last)
                    ctx.s_pe.inc(ins)

            @block.vector
            def _(v):
                for tc in range(NTC):
                    v.wait_ge(ctx.s_pe.h, pe0 + tc + 1)
                    reg = psA[:, 0:1024] if tc % 2 == 0 else psA[:, 1024:2048]
                    v.scalar_tensor_tensor(ctx.X[:, tc, :], reg, 0.5, ctx.X[:, tc, :],
                                           ALU.mult, ALU.add)
                    ctx.s_dv.inc(v.tensor_tensor(ctx.X[:, tc, :], ctx.X[:, tc, :],
                                                 ctx.bchalf[:, :], ALU.add), 2)


def emit_l1attn(ctx, xlntok):
    nc = ctx.nc
    ub = ctx.unit_base[("l1attn",)]
    uq_a, uq_b = ctx.ring[ub % NRING], ctx.ring[(ub + 1) % NRING]
    with ExitStack() as ph:
        uid = ctx.uid()
        Gtmp = _sb(ctx, f"Gtmp{uid}", [P, D], BF16, stack=ph)
        qtmp = _sb(ctx, f"qtmp{uid}", [P, T], BF16, stack=ph)
        psG = _ps(ctx, f"psG{uid}", [P, 1024], F32, stack=ph)
        psQ = _ps(ctx, f"psQ{uid}", [P, 1024], F32, stack=ph)
        psO = [_ps(ctx, f"psO{i}_{uid}", [P, 1024], F32, stack=ph) for i in range(2)]
        with nc.Block() as block:
            pe0 = ctx.s_pe.v
            dv0 = ctx.s_dv.v
            ac0 = ctx.s_ac.v

            @block.gpsimd
            def _(g):
                emit_unit_dmas(ctx, g, ctx.units_issued, ub + 2)

            @block.tensor
            def _(t):
                uwait(t, ctx, ub)
                uwait(t, ctx, ub + 1)
                for ec in range(NDC):
                    base = pe0 + ec * 19
                    if ec >= 1:
                        t.wait_ge(ctx.s_ac.h, ac0 + 3 * ec)
                    for tc in range(NTC):
                        last = tc == NTC - 1
                        ins0 = t.matmul(psG[:, 0:512], xlntok[:, tc, ec * 128:(ec + 1) * 128],
                                        xlntok[:, tc, 0:512], start=(tc == 0), stop=last)
                        if ec == 0 and tc == 0:
                            attach_uf(ctx, ins0)
                        ins = t.matmul(psG[:, 512:1024],
                                       xlntok[:, tc, ec * 128:(ec + 1) * 128],
                                       xlntok[:, tc, 512:1024], start=(tc == 0), stop=last)
                    ctx.s_pe.inc(ins)
                    usrc = uq_a if ec < 4 else uq_b
                    for th in range(2):
                        if th == 1:
                            t.wait_ge(ctx.s_ac.h, ac0 + 3 * ec + 2)
                        for dc in range(NDC):
                            last = dc == NDC - 1
                            t.matmul(psQ[:, 0:512],
                                     usrc[:, dc, (ec % 4) * 128:(ec % 4 + 1) * 128],
                                     ctx.xlnT[:, dc, th * 1024:th * 1024 + 512],
                                     start=(dc == 0), stop=last)
                            ins = t.matmul(psQ[:, 512:1024],
                                           usrc[:, dc, (ec % 4) * 128:(ec % 4 + 1) * 128],
                                           ctx.xlnT[:, dc, th * 1024 + 512:th * 1024 + 1024],
                                           start=(dc == 0), stop=last)
                        ctx.s_pe.inc(ins)
                    if ec == NDC - 1:
                        uf_free(ctx, 2)
                    t.wait_ge(ctx.s_ac.h, ac0 + 3 * ec + 3)
                    for tc in range(NTC):
                        j = ec * NTC + tc
                        if j >= 2:
                            t.wait_ge(ctx.s_dv.h, dv0 + j - 1)
                        for ns in range(2):
                            ins = t.matmul(psO[j % 2][:, ns * 512:(ns + 1) * 512],
                                           qtmp[:, tc * 128:(tc + 1) * 128],
                                           Gtmp[:, ns * 512:(ns + 1) * 512],
                                           start=True, stop=True)
                        ctx.s_pe.inc(ins)

            @block.scalar
            def _(sc):
                for ec in range(NDC):
                    base = pe0 + ec * 19
                    sc.wait_ge(ctx.s_pe.h, base + 1)
                    ctx.s_ac.inc(sc.copy(Gtmp[:, :], psG[:, :]))
                    sc.wait_ge(ctx.s_pe.h, base + 2)
                    ctx.s_ac.inc(sc.copy(qtmp[:, 0:1024], psQ[:, :]))
                    sc.wait_ge(ctx.s_pe.h, base + 3)
                    ctx.s_ac.inc(sc.copy(qtmp[:, 1024:2048], psQ[:, :]))

            @block.vector
            def _(v):
                for ec in range(NDC):
                    for tc in range(NTC):
                        j = ec * NTC + tc
                        v.wait_ge(ctx.s_pe.h, pe0 + ec * 19 + 3 + tc + 1)
                        ctx.s_dv.inc(v.tensor_tensor(ctx.X[:, tc, :], ctx.X[:, tc, :],
                                                     psO[j % 2][:, :], ALU.add))


def emit_setup(ctx):
    nc = ctx.nc
    with nc.Block() as block:
        @block.sync
        def _(s):
            ctx.s_ld.inc(s.dma_start(ctx.X[:, :, :],
                                     ctx.d_x.ap().rearrange("(c p) d -> p c d", p=P)), 16)
            ctx.s_ld.inc(s.dma_start(ctx.ident_f32[:, :], ctx.d_idf32.ap()), 16)
            ctx.s_ld.inc(s.dma_start(ctx.iota8_bc[:, :], ctx.d_iota8.ap()), 16)
            ctx.s_ld.inc(s.dma_start(ctx.ones1x64[:, :], ctx.d_ones64.ap()), 16)
            ctx.ld_setup = ctx.s_ld.v

        @block.gpsimd
        def _(g):
            ctx.s_cn.inc(g.dma_start(ctx.ident_bf[:, :], ctx.d_idf32.ap()), 16)
            ctx.s_cn.inc(g.dma_start(
                ctx.rwT_bf[:, :, :],
                ctx.d_rwT.ap().rearrange("(c p) e -> p c e", p=P)), 16)
            g.memset(ctx.negC[:, :], -STAB_C)
            g.memset(ctx.zerot[:, :], 0.0)
            g.memset(ctx.epsc[:, :], 1e-5)
            emit_unit_dmas(ctx, g, 0, min(NRING, len(ctx.units)))


def emit_store(ctx):
    nc = ctx.nc
    with nc.Block() as block:
        @block.sync
        def _(s):
            tgt = ctx.s_st.inc(s.dma_start(
                ctx.d_out.ap().rearrange("(c p) d -> p c d", p=P), ctx.X[:, :, :]), 16)
            s.wait_ge(ctx.s_st.h, tgt)


def build_program(plan=None):
    if plan is None:
        plan = FULL_PLAN
    nc = bass.Bass("TRN2", target_bir_lowering=False, debug=False)
    ctx = Ctx()
    ctx.nc = nc
    ctx._uid = 0

    def uid():
        ctx._uid += 1
        return ctx._uid
    ctx.uid = uid

    ctx.d_x = nc.dram_tensor("x_in", [T, D], F32, kind="ExternalInput")
    ctx.d_w1T = nc.dram_tensor("w1T", [NE, D, 2 * FH], F32, kind="ExternalInput")
    ctx.d_w2T = nc.dram_tensor("w2T", [NE, FH, D], F32, kind="ExternalInput")
    ctx.d_rwT = nc.dram_tensor("rwT", [D, 8], F32, kind="ExternalInput")
    ctx.d_rb_bc = nc.dram_tensor("rb_bc", [P, 8], F32, kind="ExternalInput")
    ctx.d_b1cols = nc.dram_tensor("b1cols", [P, NE, 64], F32, kind="ExternalInput")
    ctx.d_b2 = nc.dram_tensor("b2", [NE, D], F32, kind="ExternalInput")
    ctx.d_WvT = nc.dram_tensor("WvT", [D, D], F32, kind="ExternalInput")
    ctx.d_bv_cols = nc.dram_tensor("bv_cols", [P, 8], F32, kind="ExternalInput")
    ctx.d_WoTd = nc.dram_tensor("WoTd", [P, 16, D], F32, kind="ExternalInput")
    ctx.d_bo_bc = nc.dram_tensor("bo_bc", [P, D], F32, kind="ExternalInput")
    ctx.d_QT = nc.dram_tensor("QT", [D, D], F32, kind="ExternalInput")
    ctx.d_WT = nc.dram_tensor("WT_l1", [32, D, D], F32, kind="ExternalInput")
    ctx.d_bl1 = nc.dram_tensor("b_l1", [32, D], F32, kind="ExternalInput")
    ctx.d_ids = nc.dram_tensor("ids_l1", [8, 1], I32, kind="ExternalInput")
    ctx.d_wts = nc.dram_tensor("wts_l1", [8, 1], F32, kind="ExternalInput")
    ctx.d_idf32 = nc.dram_tensor("ident_f32", [P, P], F32, kind="ExternalInput")
    ctx.d_iota8 = nc.dram_tensor("iota8", [P, 8], F32, kind="ExternalInput")
    ctx.d_iota32 = nc.dram_tensor("iota32", [8, 32], F32, kind="ExternalInput")
    ctx.d_ones8 = nc.dram_tensor("ones8x128", [8, P], F32, kind="ExternalInput")
    ctx.d_ones64 = nc.dram_tensor("ones1x64", [1, 64], F32, kind="ExternalInput")
    ctx.d_onesrow = nc.dram_tensor("ones_row", [1, T], F32, kind="ExternalInput")
    ctx.d_out = nc.dram_tensor("x_out", [T, D], F32, kind="ExternalOutput")
    if DEBUG_ATTN:
        ctx.d_dbg_pT = nc.dram_tensor("dbg_pT", [P, NDC, T], F32, kind="ExternalOutput")
        ctx.d_dbg_pT2 = nc.dram_tensor("dbg_pT2", [P, NDC, T], F32, kind="ExternalOutput")
        ctx.d_dbg_d2 = nc.dram_tensor("dbg_d2", [P, NTC], F32, kind="ExternalOutput")
        ctx.d_dbg_av = nc.dram_tensor("dbg_av", [65, T], F32, kind="ExternalOutput")
        ctx.d_dbg_eb = nc.dram_tensor("dbg_eb", [P, 2, T], F32, kind="ExternalOutput")
        ctx.d_dbg_vh = nc.dram_tensor("dbg_vh", [P, NTC, 72], F32, kind="ExternalOutput")
    if DEBUG_MOE:
        ctx.d_dbg_lsb = nc.dram_tensor("dbg_lsb", [P, NTC, 8], F32, kind="ExternalOutput")
        ctx.d_dbg_mx8 = nc.dram_tensor("dbg_mx8", [P, NTC, 8], F32, kind="ExternalOutput")
        ctx.d_dbg_idx = nc.dram_tensor("dbg_idx", [P, NTC, 8], F32, kind="ExternalOutput")
        ctx.d_dbg_w8c = nc.dram_tensor("dbg_w8c", [P, NTC, 8], F32, kind="ExternalOutput")
        ctx.d_dbg_wstr = nc.dram_tensor("dbg_wstr", [P, NTC, 8, 8], F32, kind="ExternalOutput")

    ctx.units = unit_schedule(plan)
    ctx.unit_base = {}
    k = 0
    for st in plan:
        if st == "moe0":
            ctx.unit_base[("moe", 0)] = k
            k += 24 * NE
        elif st == "moe1":
            ctx.unit_base[("moe", 1)] = k
            k += 24 * NE
        elif st == "attn":
            ctx.unit_base[("attn",)] = k
            k += 6
        elif st == "l1attn":
            ctx.unit_base[("l1attn",)] = k
            k += 2
    ctx.units_issued = 0
    ctx.pending_uf = 0

    with ExitStack() as stack:
        ctx.stack = stack
        for name in ["s_ld", "s_un", "s_uf", "s_pe", "s_m1", "s_e1", "s_hr",
                     "s_m2", "s_e2", "s_ac", "s_dv", "s_st", "s_dvb", "s_cn", "s_ww"]:
            setattr(ctx, name, Sem(stack, nc, name))

        ctx.X = _sb(ctx, "Xres", [P, NTC, D], F32)
        ctx.xlnT = _sb(ctx, "xlnT", [P, NDC, T], BF16)
        ctx.ring = [_sb(ctx, f"ring{i}", [P, 8, 512], BF16) for i in range(NRING)]
        ctx.rwT_bf = _sb(ctx, "rwT_bf", [P, 8, 8], BF16)
        ctx.ident_bf = _sb(ctx, "identbf", [P, P], BF16)
        ctx.ident_f32 = _sb(ctx, "identf32", [P, P], F32)
        ctx.iota8_bc = _sb(ctx, "iota8bc", [P, 8], F32)
        ctx.negC = _sb(ctx, "negC", [P, 1], F32)
        ctx.zerot = _sb(ctx, "zerot", [P, 1], F32)
        ctx.epsc = _sb(ctx, "epsc", [P, 1], F32)
        ctx.ones1x64 = _sb(ctx, "ones1x64b", [1, 64], F32)
        ctx.d2scr = _sb(ctx, "d2scr", [P, 64], F32)
        ctx.s1buf = _sb(ctx, "s1buf", [P, NTC], F32)
        ctx.s2buf = _sb(ctx, "s2buf", [P, NTC], F32)
        ctx.msqbuf = _sb(ctx, "msqbuf", [P, NTC], F32)
        ctx.varbuf = _sb(ctx, "varbuf", [P, NTC], F32)
        ctx.stdbuf = _sb(ctx, "stdbuf", [P, NTC], F32)
        ctx.mstr = _sb(ctx, "mstr", [P, NTC * 8], F32)
        ctx.rstdstr = _sb(ctx, "rstdstr", [P, NTC * 8], F32)

        emit_setup(ctx)
        i = 0
        while i < len(plan):
            st = plan[i]
            if st == "ln":
                emit_ln(ctx)
            elif st == "lng":
                with ExitStack() as gsc:
                    xtok = _sb(ctx, f"xtokg{ctx.uid()}", [P, NTC, D], BF16, stack=gsc)
                    emit_ln(ctx, xlntok=xtok)
                    if i + 1 < len(plan) and plan[i + 1] == "l1attn":
                        emit_l1attn(ctx, xtok)
                        i += 1
            elif st == "moe0":
                emit_moe(ctx, 0)
            elif st == "moe1":
                emit_moe(ctx, 1)
            elif st == "attn":
                emit_attn(ctx)
            elif st == "l1prep":
                emit_l1prep(ctx)
            elif st == "l1ffn":
                emit_l1ffn(ctx)
            elif st == "l1attn":
                raise RuntimeError("l1attn must follow lng")
            i += 1
        emit_store(ctx)
    return nc


PLAN_OVERRIDE = None
ATTN_EVT = None
DEBUG_MOE = False
DEBUG_ATTN = False
DEBUG_HEAD = 0


def kernel(L0_attn_x, L0_attn_Wv, L0_attn_bv, L0_attn_Wo, L0_attn_bo,
           L0_ffn_router_w, L0_ffn_router_b, L0_ffn_w1, L0_ffn_b1, L0_ffn_w2,
           L0_ffn_b2, L1_attn_queries, L1_ffn_W, L1_ffn_b, L1_ffn_ids, L1_ffn_weights):
    plan = PLAN_OVERRIDE if PLAN_OVERRIDE is not None else FULL_PLAN
    f32 = np.float32

    def asf(a):
        return np.ascontiguousarray(np.asarray(a), dtype=f32)

    x = asf(L0_attn_x)
    w1T = np.ascontiguousarray(asf(L0_ffn_w1).transpose(0, 2, 1))
    w2T = np.ascontiguousarray(asf(L0_ffn_w2).transpose(0, 2, 1))
    rwT = np.ascontiguousarray(asf(L0_ffn_router_w).T)
    rb_bc = np.ascontiguousarray(np.broadcast_to(asf(L0_ffn_router_b), (P, 8)))
    b1cols = np.ascontiguousarray(asf(L0_ffn_b1).reshape(NE, 64, P).transpose(2, 0, 1))
    b2 = asf(L0_ffn_b2)
    WvT = np.ascontiguousarray(asf(L0_attn_Wv).T)
    bv = asf(L0_attn_bv)
    bv_cols = np.ascontiguousarray(bv.reshape(8, P).T)
    WoT = np.ascontiguousarray(asf(L0_attn_Wo).T)
    WoTd = np.empty((P, 16, D), dtype=f32)
    for j in range(16):
        blk = WoT[j * 64:(j + 1) * 64, :]
        WoTd[0:64, j] = blk
        WoTd[64:128, j] = blk
    bo_bc = np.ascontiguousarray(np.broadcast_to(asf(L0_attn_bo), (P, D)))
    QT = np.ascontiguousarray(asf(L1_attn_queries).T)
    WT = np.ascontiguousarray(asf(L1_ffn_W).transpose(0, 2, 1))
    bl1 = asf(L1_ffn_b)
    ids = np.ascontiguousarray(np.asarray(L1_ffn_ids).astype(np.int32).reshape(8, 1))
    wts = asf(L1_ffn_weights).reshape(8, 1)
    ident_f32 = np.eye(P, dtype=f32)
    iota8 = np.ascontiguousarray(np.broadcast_to(np.arange(8, dtype=f32), (P, 8)))
    iota32 = np.ascontiguousarray(np.broadcast_to(np.arange(32, dtype=f32), (8, 32)))
    ones8 = np.ones((8, P), dtype=f32)
    ones64 = np.ones((1, 64), dtype=f32)
    onesrow = np.ones((1, T), dtype=f32)

    shared = dict(w1T=w1T, w2T=w2T, rwT=rwT, rb_bc=rb_bc, b1cols=b1cols, b2=b2,
                  WvT=WvT, bv_cols=bv_cols, WoTd=WoTd, bo_bc=bo_bc,
                  QT=QT, WT_l1=WT, b_l1=bl1, ids_l1=ids, wts_l1=wts,
                  ident_f32=ident_f32, iota8=iota8,
                  iota32=iota32, ones8x128=ones8, ones1x64=ones64, ones_row=onesrow)

    nc = build_program(plan)
    in_maps = [dict(shared, x_in=np.ascontiguousarray(x[i])) for i in range(8)]
    res = run_bass_kernel_spmd(nc, in_maps, core_ids=list(range(8)))
    return np.stack([r["x_out"] for r in res.results], axis=0)
